# version 07
# baseline (speedup 1.0000x reference)
import numpy as np
import ml_dtypes
import contextlib
import concourse.bass as bass
import concourse.mybir as mybir
from concourse.bass_utils import run_bass_kernel_spmd
from concourse.replica_groups import maybe_share_collective_output_space

F32 = mybir.dt.float32
BF16 = mybir.dt.bfloat16
AF = mybir.ActivationFunctionType
ALU = mybir.AluOpType
AX = mybir.AxisListType
bfloat16 = ml_dtypes.bfloat16

NCORE = 8
D = 1024
SEQ = 4096
TOWN = 2048
HD = 64
NPAIR = 8
FF = 4096
SCALE = 1.0 / 32.0
CSHIFT = 4.0
EPS = 1e-5
NT_OWN = TOWN // 128
NMT = SEQ // 128
NSUB = 32
VSTRIDE = 1056
PBLOCKS = [(g, r) for g in range(8) for r in range(8)]


class Sem:
    def __init__(self, h):
        self.h = h
        self.n = 0

    def tick(self, instr, k=1):
        instr.then_inc(self.h, k)
        self.n += k
        return self.n


class Px:

    def __init__(self, block, name):
        object.__setattr__(self, "_b", block)
        object.__setattr__(self, "_n", name)

    def __getattr__(self, attr):
        def emit(*a, **k):
            out = []

            def body(eng):
                out.append(getattr(eng, attr)(*a, **k))

            getattr(self._b, self._n)(body)
            return out[0]

        return emit


class EQ:
    def __init__(self, eng):
        self.e = eng
        self.waited = {}

    def wait(self, sem, val=None):
        v = sem.n if val is None else val
        if v <= 0 or self.waited.get(id(sem), -1) >= v:
            return
        self.e.wait_ge(sem.h, v)
        self.waited[id(sem)] = v


def build_nc():
    nc = bass.Bass("TRN2", target_bir_lowering=False, debug=False)

    x_d = nc.dram_tensor("x", [TOWN, D], F32, kind="ExternalInput").ap()
    wqT = nc.dram_tensor("wqT", [D, D], F32, kind="ExternalInput").ap()
    wkT = nc.dram_tensor("wkT", [D, D], F32, kind="ExternalInput").ap()
    wvT = nc.dram_tensor("wvT", [D, D], F32, kind="ExternalInput").ap()
    woT = nc.dram_tensor("woT", [D, D], F32, kind="ExternalInput").ap()
    w1T_d = nc.dram_tensor("w1T", [D, FF], F32, kind="ExternalInput").ap()
    w2T_d = nc.dram_tensor("w2T", [FF, D], F32, kind="ExternalInput").ap()
    wrT_d = nc.dram_tensor("wrT", [D, 8], F32, kind="ExternalInput").ap()
    bqt_d = nc.dram_tensor("bqt", [128, 8], F32, kind="ExternalInput").ap()
    bkt_d = nc.dram_tensor("bkt", [128, 8], F32, kind="ExternalInput").ap()
    b1t_d = nc.dram_tensor("b1t", [128, 32], F32, kind="ExternalInput").ap()
    brc_d = nc.dram_tensor("brc", [8, 1], F32, kind="ExternalInput").ap()
    bvb_d = nc.dram_tensor("bvb", [128, D], F32, kind="ExternalInput").ap()
    bob_d = nc.dram_tensor("bob", [128, D], F32, kind="ExternalInput").ap()
    gb_d = nc.dram_tensor("gb", [128, D], F32, kind="ExternalInput").ap()
    beb_d = nc.dram_tensor("beb", [128, D], F32, kind="ExternalInput").ap()
    b2b_d = nc.dram_tensor("b2b", [128, D], F32, kind="ExternalInput").ap()
    eye8_d = nc.dram_tensor("eye8", [8, 8], F32, kind="ExternalInput").ap()
    esel_d = nc.dram_tensor("esel", [128, 8], F32, kind="ExternalInput").ap()
    out_d = nc.dram_tensor("out", [TOWN, D], F32, kind="ExternalOutput").ap()

    rg8 = [list(range(NCORE))]
    rg2 = [[2 * b, 2 * b + 1] for b in range(4)]
    sp_ag8 = maybe_share_collective_output_space("AllGather", rg8)
    sp_ag2 = maybe_share_collective_output_space("AllGather", rg2)
    sp_rs = maybe_share_collective_output_space("ReduceScatter", rg8)
    ln0_loc = nc.dram_tensor("ln0_loc", [TOWN, D], BF16)
    ln0_pr = nc.dram_tensor("ln0_pr", [SEQ, D], BF16, addr_space=sp_ag2)
    ln1_loc = nc.dram_tensor("ln1_loc", [TOWN, D], BF16)
    ln1_ag = nc.dram_tensor("ln1_ag", [NCORE * TOWN, D], BF16, addr_space=sp_ag8)
    x1sp = nc.dram_tensor("x1sp", [TOWN, D], F32)
    contrib = nc.dram_tensor("contrib", [NCORE * TOWN, D], F32)
    rsout = nc.dram_tensor("rsout", [TOWN, D], F32, addr_space=sp_rs)
    bar_in = nc.dram_tensor("bar_in", [1, 64], F32)
    bar_out = nc.dram_tensor("bar_out", [1, 64], F32)

    SB0 = 16544
    R1 = 229344 - 67584
    R2 = R1 - 65536
    R3 = R2 - 32768
    R4 = R3 - 16384
    A = nc.alloc_sbuf_tensor_at
    s_vaug = A("s_vaug", [128, 33792], BF16, offset=R1)
    s_w1 = A("s_w1", [128, 32768], BF16, offset=R1)
    s_qt = A("s_qt", [128, 8192], BF16, offset=R2)
    s_kt = A("s_kt", [128, 4096], BF16, offset=R2 + 16384)
    s_ln0t = A("s_ln0t", [128, 8192], BF16, offset=R2 + 24576)
    s_equad = A("s_equad", [128, 8192], BF16, offset=R2 + 40960)
    s_bvb = A("s_bvb", [128, D], F32, offset=R2 + 57344)
    s_bob = A("s_bob", [128, D], F32, offset=R2 + 61440)
    s_w2 = A("s_w2", [128, 32768], BF16, offset=R2)
    s_attnT = A("s_attnT", [128, 16384], BF16, offset=R3)
    s_ht = A("s_ht", [128, 16384], BF16, offset=R3)
    s_wv = A("s_wv", [128, 8192], BF16, offset=R4)
    s_wqk = A("s_wqk", [128, 2048], BF16, offset=R4)
    s_wo = A("s_wo", [128, 8192], BF16, offset=R4)
    s_ln1t = A("s_ln1t", [128, 8192], BF16, offset=R4)

    off = [16544]

    def AL(name, shape, dtype):
        per = int(np.prod(shape[1:])) * (4 if dtype == F32 else 2)
        t = A(name, shape, dtype, offset=off[0])
        off[0] += (per + 31) // 32 * 32
        return t

    s_xt = AL("s_xt", [128, D], F32)
    s_s1 = AL("s_s1", [128, D], F32)
    s_s2 = AL("s_s2", [128, D], F32)
    s_ln1s = AL("s_ln1s", [128, D], BF16)
    s_gb = AL("s_gb", [128, D], BF16)
    s_beb = AL("s_beb", [128, D], BF16)
    s_b2b = AL("s_b2b", [128, D], BF16)
    s_c1 = AL("s_c1", [128, D], F32)
    s_r = AL("s_r", [1, 512], BF16)
    s_st = AL("s_st", [128, 32], F32)
    s_sc = AL("s_sc", [128, 8], F32)
    s_lgT = AL("s_lgT", [8, 512], F32)
    s_lg = AL("s_lg", [128, 8], F32)
    s_ra = AL("s_ra", [128, 8], F32)
    s_rb = AL("s_rb", [128, 8], F32)
    s_rm = AL("s_rm", [128, 8], F32)
    s_wcol = AL("s_wcol", [128, 4], F32)
    s_esel = AL("s_esel", [128, 8], F32)
    s_bq = AL("s_bq", [128, 8], F32)
    s_bk = AL("s_bk", [128, 8], F32)
    s_b1t = AL("s_b1t", [128, 32], F32)
    s_br = AL("s_br", [8, 1], F32)
    s_eye = AL("s_eye", [8, 8], F32)
    s_ones64 = AL("s_ones64", [1, 64], BF16)
    s_eps = AL("s_eps", [128, 1], F32)
    s_csh = AL("s_csh", [128, 1], F32)
    s_wr = AL("s_wr", [128, 64], BF16)
    assert off[0] <= R4, f"misc overflow {off[0]} > {R4}"

    PA = nc.alloc_psum_tensor("PA", [128, 1024], F32)
    PB = nc.alloc_psum_tensor("PB", [128, 1024], F32)
    PC = nc.alloc_psum_tensor("PC", [128, 512], F32)
    PD = nc.alloc_psum_tensor("PD", [128, 512], F32)
    PE_ = nc.alloc_psum_tensor("PEb", [128, 512], F32)
    PF = nc.alloc_psum_tensor("PF", [128, 512], F32)
    SROT = [PC, PD, PE_, PF]

    es = contextlib.ExitStack()
    sQ = Sem(es.enter_context(nc.semaphore("sQ")))
    sA = Sem(es.enter_context(nc.semaphore("sA")))
    sG = Sem(es.enter_context(nc.semaphore("sG")))
    sCT = Sem(es.enter_context(nc.semaphore("sCT")))
    sST = Sem(es.enter_context(nc.semaphore("sST")))
    sCC = Sem(es.enter_context(nc.semaphore("sCC")))
    sPE = Sem(es.enter_context(nc.semaphore("sPE")))
    sAC = Sem(es.enter_context(nc.semaphore("sAC")))
    sVE = Sem(es.enter_context(nc.semaphore("sVE")))
    sT0 = Sem(es.enter_context(nc.semaphore("sT0")))
    sT1 = Sem(es.enter_context(nc.semaphore("sT1")))
    sW = Sem(es.enter_context(nc.semaphore("sW")))
    sC0 = Sem(es.enter_context(nc.semaphore("sC0")))
    sC1 = Sem(es.enter_context(nc.semaphore("sC1")))
    sT = [sT0, sT1]
    sC = [sC0, sC1]
    block = es.enter_context(nc.Block())

    with es:
        sy = Px(block, "sync")
        gp = Px(block, "gpsimd")
        pe = Px(block, "tensor")
        ac = Px(block, "scalar")
        ve = Px(block, "vector")
        Esy, Egp, Epe, Eac, Eve = EQ(sy), EQ(gp), EQ(pe), EQ(ac), EQ(ve)

        def vstep(instr):
            sVE.tick(instr)
            Eve.wait(sVE)

        def astep(instr):
            sAC.tick(instr)
            Eac.wait(sAC)

        def ln_tile(x_ap, scr_ap, out_ap, negmu_ap, rstd_ap):
            vstep(ve.reduce_sum(s_sc[:, 0:1], x_ap, axis=AX.X))
            vstep(ve.tensor_scalar(negmu_ap, s_sc[:, 0:1], -1.0 / D, None, ALU.mult))
            vstep(ve.tensor_scalar(scr_ap, x_ap, negmu_ap, None, ALU.add))
            vstep(ve.scalar_tensor_tensor(s_c1[:, :], scr_ap, 1.0, scr_ap,
                                          ALU.mult, ALU.mult,
                                          accum_out=s_sc[:, 1:2]))
            vstep(ve.tensor_scalar(s_sc[:, 1:2], s_sc[:, 1:2], EPS * D, None,
                                   ALU.add))
            Eac.wait(sVE)
            astep(ac.activation(s_sc[:, 2:3], s_sc[:, 1:2], AF.Ln, bias=0.0,
                                scale=1.0 / D))
            astep(ac.activation(rstd_ap, s_sc[:, 2:3], AF.Exp, bias=0.0, scale=-0.5))
            Eve.wait(sAC)
            vstep(ve.tensor_scalar(out_ap, scr_ap, rstd_ap, None, ALU.mult))

        def afence():
            astep(ac.activation(s_sc[:, 6:7], s_sc[:, 6:7], AF.Exp, scale=0.0))

        def vfence():
            vstep(ve.tensor_copy(s_sc[:, 7:8], s_sc[:, 7:8]))

        vstep(ve.memset(s_ones64[:, :], 1.0))
        vstep(ve.memset(s_sc[:, :], 0.0))
        vstep(ve.memset(s_st[:, :], 0.0))
        vstep(ve.memset(s_eps[:, :], EPS))
        vstep(ve.memset(s_csh[:, :], -CSHIFT))
        for _mi in range(4):
            vstep(ve.memset(s_vaug[:, _mi * 8448:(_mi + 1) * 8448], 1.0))
        for dst, src in [(s_bq, bqt_d), (s_bk, bkt_d), (s_b1t, b1t_d), (s_br, brc_d),
                         (s_eye, eye8_d), (s_bvb, bvb_d), (s_bob, bob_d),
                         (s_esel, esel_d)]:
            sQ.tick(sy.dma_start(out=dst[:, :], in_=src), 16)
        for dst, src in [(s_gb, gb_d), (s_beb, beb_d), (s_b2b, b2b_d)]:
            sG.tick(gp.dma_start(out=dst[:, :], in_=src), 16)
        for c in range(8):
            sG.tick(gp.dma_start(out=s_wv[:, c * 1024:(c + 1) * 1024],
                                 in_=wvT[c * 128:(c + 1) * 128, :]), 16)
        for dblk in range(8):
            sG.tick(gp.dma_start(out=s_wr[:, dblk * 8:(dblk + 1) * 8],
                                 in_=wrT_d[dblk * 128:(dblk + 1) * 128, :]), 16)

        for i in range(NT_OWN):
            Esy.wait(sVE, sVE.n)
            sQ.tick(sy.dma_start(out=s_xt[:, :], in_=x_d[i * 128:(i + 1) * 128, :]), 16)
            Eve.wait(sQ)
            Eac.wait(sQ)
            ln_tile(s_xt[:, :], s_s1[:, :], s_ln1s[:, :], s_sc[:, 4:5], s_sc[:, 5:6])
            vfence()
            Esy.wait(sVE)
            sQ.tick(sy.dma_start(out=ln0_loc.ap()[i * 128:(i + 1) * 128, :],
                                 in_=s_ln1s[:, :]), 16)
        sG.tick(gp.dma_start(out=bar_in.ap(), in_=x_d[0:1, 0:64]), 16)
        Egp.wait(sQ)
        Egp.wait(sG)
        sCC.tick(gp.collective_compute(
            "AllGather", ALU.bypass, replica_groups=rg2,
            ins=[ln0_loc.ap().opt()], outs=[ln0_pr.ap().opt()]))
        Egp.wait(sCC)
        sCC.tick(gp.collective_compute(
            "AllReduce", ALU.add, replica_groups=rg8,
            ins=[bar_in.ap().opt()], outs=[bar_out.ap().opt()]))
        CC_AG0 = sCC.n

        ln0t_free = [0, 0]

        def load_ln0t(mc, buf, rows_src):
            Esy.wait(sPE, ln0t_free[buf])
            Eac.wait(sPE, ln0t_free[buf])
            base = buf * 4096
            for c in range(8):
                eng = sy
                sT[buf].tick(eng.dma_start(
                    out=s_ln0t[:, base + c * 512: base + (c + 1) * 512],
                    in_=rows_src[mc * 512:(mc + 1) * 512, c * 128:(c + 1) * 128],
                    transpose=True), 16)

        Esy.wait(sCC, CC_AG0)
        Eac.wait(sCC, CC_AG0)
        vp_evac = []
        for mc in range(8):
            buf = mc % 2
            load_ln0t(mc, buf, ln0_pr.ap())
            Epe.wait(sT[buf])
            Epe.wait(sG)
            for mt in range(4):
                m = mc * 4 + mt
                P = PA if (m % 2 == 0) else PB
                if m >= 2:
                    Epe.wait(sVE, vp_evac[m - 2])
                for c in range(8):
                    lhs = s_ln0t[:, buf * 4096 + c * 512 + mt * 128:
                                 buf * 4096 + c * 512 + (mt + 1) * 128]
                    rhs = s_wv[:, c * 1024:(c + 1) * 1024]
                    pe.matmul(P[:, 0:512], lhs, rhs[:, 0:512],
                              start=(c == 0), stop=(c == 7))
                    last = pe.matmul(P[:, 512:1024], lhs, rhs[:, 512:1024],
                                     start=(c == 0), stop=(c == 7))
                sPE.tick(last)
                Eve.wait(sPE)
                vstep(ve.tensor_add(s_s1[:, :], P[:, :], s_bvb[:, :]))
                for hh in range(16):
                    cp = ve.tensor_copy(
                        s_vaug[:, m * VSTRIDE + hh * 66: m * VSTRIDE + hh * 66 + 64],
                        s_s1[:, hh * 64:(hh + 1) * 64])
                sVE.tick(cp)
                Eve.wait(sVE)
                vfence()
                vp_evac.append(sVE.n)
            ln0t_free[buf] = sPE.n
        V_READY_VE = sVE.n

        qk_free_pe = [sPE.n]
        qk_evac = []

        def qk_proj_pair(p):
            Epe.wait(sVE, V_READY_VE)
            Egp.wait(sPE, qk_free_pe[0])
            for c in range(8):
                sG.tick(gp.dma_start(out=s_wqk[:, c * 128:(c + 1) * 128],
                                     in_=wqT[c * 128:(c + 1) * 128,
                                             p * 128:(p + 1) * 128]), 16)
                sG.tick(gp.dma_start(out=s_wqk[:, 1024 + c * 128:1024 + (c + 1) * 128],
                                     in_=wkT[c * 128:(c + 1) * 128,
                                             p * 128:(p + 1) * 128]), 16)
            q = p % 2
            for j in range(12):
                is_q = j < 8
                mc = j if is_q else j - 8
                buf = j % 2
                load_ln0t(mc, buf, ln0_pr.ap() if is_q else ln0_loc.ap())
                Epe.wait(sT[buf])
                Epe.wait(sG)
                P = PA if (j % 2 == 0) else PB
                if len(qk_evac) >= 2:
                    Epe.wait(sVE, qk_evac[-2])
                woff = 0 if is_q else 1024
                for c in range(8):
                    last = pe.matmul(
                        P[:, 0:512],
                        s_wqk[:, woff + c * 128: woff + (c + 1) * 128],
                        s_ln0t[:, buf * 4096 + c * 512: buf * 4096 + (c + 1) * 512],
                        start=(c == 0), stop=(c == 7))
                sPE.tick(last)
                Eve.wait(sPE)
                if is_q:
                    dst = s_qt[:, q * 4096 + mc * 512: q * 4096 + (mc + 1) * 512]
                    bias = s_bq[:, p:p + 1]
                else:
                    dst = s_kt[:, q * 2048 + mc * 512: q * 2048 + (mc + 1) * 512]
                    bias = s_bk[:, p:p + 1]
                vstep(ve.tensor_scalar(dst, P[:, 0:512], bias, None, ALU.add))
                qk_evac.append(sVE.n)
                ln0t_free[buf] = sPE.n
            vfence()
            qk_free_pe[0] = sPE.n

        ATTN_VE = [0]

        def attn_pair(p):
            q = p % 2
            Epe.wait(sVE)
            Epe.wait(sVE, V_READY_VE)
            for nch in range(4):
                nbase = nch * 512
                Epe.wait(sVE)
                s_done = []
                for quad in range(4):
                    for mi in range(8):
                        m = quad * 8 + mi
                        for hh in range(2):
                            k = mi * 2 + hh
                            n_op = quad * 16 + k
                            bank = SROT[n_op % 4]
                            if n_op >= 4:
                                Epe.wait(sAC, s_done[n_op - 4])
                            lhs = s_qt[hh * 64:(hh + 1) * 64,
                                       q * 4096 + m * 128: q * 4096 + (m + 1) * 128]
                            rhs = s_kt[hh * 64:(hh + 1) * 64,
                                       q * 2048 + nbase: q * 2048 + nbase + 512]
                            mm = pe.matmul(bank[:, :], lhs, rhs, start=True,
                                           stop=True, tile_position=(hh * 64, 0))
                            sPE.tick(mm)
                            Eac.wait(sPE)
                            ex = ac.activation(s_equad[:, k * 512:(k + 1) * 512],
                                               bank[:, :], AF.Exp,
                                               bias=0.0, scale=SCALE)
                            s_done.append(sAC.tick(ex))
                    afence()
                    fq = sAC.n
                    for mi in range(8):
                        m = quad * 8 + mi
                        for hh in range(2):
                            k = mi * 2 + hh
                            Epe.wait(sAC, fq)
                            lhs = s_vaug[:, m * VSTRIDE + (p * 2 + hh) * 66:
                                         m * VSTRIDE + (p * 2 + hh) * 66 + 65]
                            mm = pe.matmul(PB[0:65, hh * 512:(hh + 1) * 512], lhs,
                                           s_equad[:, k * 512:(k + 1) * 512],
                                           start=(m == 0), stop=(m == NMT - 1))
                    sPE.tick(mm)
                for hh in range(2):
                    Eve.wait(sPE)
                    vstep(ve.tensor_copy(s_s2[0:65, 0:512],
                                         PB[0:65, hh * 512:(hh + 1) * 512]))
                    with nc.allow_low_precision(
                            reason="softmax 1/sum stored bf16 for the rank-1 "
                                   "broadcast matmul; same error class as bf16 "
                                   "matmul operands"):
                        vstep(ve.reciprocal(s_r[:, :], s_s2[64:65, 0:512]))
                    vfence()
                    Epe.wait(sVE)
                    mm = pe.matmul(PC[0:64, :], s_ones64[:, :], s_r[:, :],
                                   start=True, stop=True)
                    sPE.tick(mm)
                    Eve.wait(sPE)
                    dst = s_attnT[hh * 64:(hh + 1) * 64,
                                  p * 2048 + nbase: p * 2048 + nbase + 512]
                    vstep(ve.tensor_mul(dst, s_s2[0:64, 0:512], PC[0:64, :]))
            vfence()
            ATTN_VE[0] = sVE.n

        for p in range(NPAIR):
            qk_proj_pair(p)
            attn_pair(p)
        QT_DEAD_PE = sPE.n

        Egp.wait(sPE, qk_free_pe[0])
        for c in range(8):
            sG.tick(gp.dma_start(out=s_wo[:, c * 1024:(c + 1) * 1024],
                                 in_=woT[c * 128:(c + 1) * 128, :]), 16)
        WO_G = sG.n
        Egp.wait(sPE, QT_DEAD_PE)
        for dblk in range(8):
            sW.tick(gp.dma_start(out=s_w1[:, dblk * 4096:(dblk + 1) * 4096],
                                 in_=w1T_d[dblk * 128:(dblk + 1) * 128, :]), 16)
        for fb in range(30):
            sW.tick(gp.dma_start(out=s_w2[:, fb * 1024:(fb + 1) * 1024],
                                 in_=w2T_d[fb * 128:(fb + 1) * 128, :]), 16)

        Epe.wait(sVE, ATTN_VE[0])
        Epe.wait(sG, WO_G)
        pa_ok = 0
        for i in range(NT_OWN):
            Epe.wait(sVE, pa_ok)
            for fb in range(8):
                lhs = s_attnT[:, fb * 2048 + i * 128: fb * 2048 + (i + 1) * 128]
                rhs = s_wo[:, fb * 1024:(fb + 1) * 1024]
                pe.matmul(PA[:, 0:512], lhs, rhs[:, 0:512],
                          start=(fb == 0), stop=(fb == 7))
                mm = pe.matmul(PA[:, 512:1024], lhs, rhs[:, 512:1024],
                               start=(fb == 0), stop=(fb == 7))
            sPE.tick(mm)
            Esy.wait(sVE)
            sQ.tick(sy.dma_start(out=s_xt[:, :], in_=x_d[i * 128:(i + 1) * 128, :]), 16)
            Eve.wait(sPE)
            Eve.wait(sQ)
            vstep(ve.tensor_add(s_s1[:, :], PA[:, :], s_bob[:, :]))
            pa_ok = sVE.n
            ln_tile(s_s1[:, :], s_s2[:, :], s_s2[:, :], s_sc[:, 4:5], s_sc[:, 5:6])
            Eve.wait(sQ)
            vstep(ve.tensor_add(s_s1[:, :], s_xt[:, :], s_s2[:, :]))
            vfence()
            Esy.wait(sVE)
            sQ.tick(sy.dma_start(out=x1sp.ap()[i * 128:(i + 1) * 128, :],
                                 in_=s_s1[:, :]), 16)
            ln_tile(s_s1[:, :], s_s2[:, :], s_ln1s[:, :],
                    s_st[:, 2 * i:2 * i + 1], s_st[:, 2 * i + 1:2 * i + 2])
            vfence()
            Esy.wait(sVE)
            sQ.tick(sy.dma_start(out=ln1_loc.ap()[i * 128:(i + 1) * 128, :],
                                 in_=s_ln1s[:, :]), 16)
        OUTPROJ_PE = sPE.n
        Egp.wait(sVE)
        for fb in (30, 31):
            sW.tick(gp.dma_start(out=s_w2[:, fb * 1024:(fb + 1) * 1024],
                                 in_=w2T_d[fb * 128:(fb + 1) * 128, :]), 16)
        W12_W = sW.n
        Egp.wait(sQ)
        sCC.tick(gp.collective_compute(
            "AllGather", ALU.bypass, replica_groups=rg8,
            ins=[ln1_loc.ap().opt()], outs=[ln1_ag.ap().opt()]))
        Egp.wait(sCC)
        sCC.tick(gp.collective_compute(
            "AllReduce", ALU.add, replica_groups=rg8,
            ins=[bar_in.ap().opt()], outs=[bar_out.ap().opt()]))
        CC_AG1 = sCC.n

        ln1t_free = [OUTPROJ_PE, OUTPROJ_PE]
        pa_free_ve = [0, 0]
        rs_done = []
        phasea_evac = []

        def load_ln1t(s, buf):
            Esy.wait(sPE, ln1t_free[buf])
            Eac.wait(sPE, ln1t_free[buf])
            for bi in range(2):
                g, r = PBLOCKS[2 * s + bi]
                rows = ln1_ag.ap()[r * TOWN + g * 256: r * TOWN + (g + 1) * 256, :]
                for c in range(8):
                    eng = sy
                    sT[buf].tick(eng.dma_start(
                        out=s_ln1t[:, buf * 4096 + c * 512 + bi * 256:
                                   buf * 4096 + c * 512 + bi * 256 + 256],
                        in_=rows[:, c * 128:(c + 1) * 128], transpose=True), 16)

        def ffn_epilogue(cidx):
            Esy.wait(sCC, rs_done[cidx])
            Eve.wait(sC[0])
            Eve.wait(sC[1])
            for k in range(2):
                i = cidx * 2 + k
                Esy.wait(sVE)
                sQ.tick(sy.dma_start(out=s_xt[:, :],
                                     in_=x1sp.ap()[i * 128:(i + 1) * 128, :]), 16)
                sQ.tick(sy.dma_start(out=s_s2[:, :],
                                     in_=rsout.ap()[i * 128:(i + 1) * 128, :]), 16)
                Eve.wait(sQ)
                Eve.wait(sST)
                vstep(ve.tensor_scalar(s_s1[:, :], s_xt[:, :],
                                       s_st[:, 2 * i:2 * i + 1],
                                       s_st[:, 2 * i + 1:2 * i + 2],
                                       ALU.add, ALU.mult))
                vstep(ve.tensor_add(s_s1[:, :], s_s1[:, :], s_s2[:, :]))
                ln_tile(s_s1[:, :], s_s2[:, :], s_s2[:, :], s_sc[:, 4:5],
                        s_sc[:, 5:6])
                vstep(ve.tensor_mul(s_s2[:, :], s_s2[:, :], s_gb[:, :]))
                vstep(ve.tensor_add(s_s2[:, :], s_s2[:, :], s_beb[:, :]))
                ln_tile(s_s2[:, :], s_s1[:, :], s_s1[:, :], s_sc[:, 4:5],
                        s_sc[:, 5:6])
                vstep(ve.tensor_add(s_s1[:, :], s_s1[:, :], s_xt[:, :]))
                vfence()
                Eac.wait(sVE)
                sST.tick(ac.dma_start(out=out_d[i * 128:(i + 1) * 128, :],
                                      in_=s_s1[:, :]), 16)

        Esy.wait(sCC, CC_AG1)
        Eac.wait(sCC, CC_AG1)
        Epe.wait(sW, W12_W)
        Epe.wait(sVE)
        Epe.wait(sAC)

        for s in range(NSUB):
            buf = s % 2
            load_ln1t(s, buf)
            Epe.wait(sT[buf])
            lnb = s_ln1t[:, buf * 4096:(buf + 1) * 4096]

            Epe.wait(sAC)
            for dblk in range(8):
                mm = pe.matmul(PE_[0:8, :], s_wr[:, dblk * 8:(dblk + 1) * 8],
                               lnb[:, dblk * 512:(dblk + 1) * 512],
                               start=(dblk == 0), stop=(dblk == 7))
            sPE.tick(mm)
            Eve.wait(sPE)
            vstep(ve.tensor_scalar(s_lgT[:, :], PE_[0:8, :], s_br[:, :], None,
                                   ALU.add))
            vfence()
            for t in range(4):
                Epe.wait(sVE)
                mm = pe.transpose(PF[:, 0:8], s_lgT[:, t * 128:(t + 1) * 128],
                                  s_eye[:, :])
                sPE.tick(mm)
                Eve.wait(sPE)
                vstep(ve.tensor_copy(s_lg[:, :], PF[:, 0:8]))
                vstep(ve.reduce_max(s_rm[:, 0:1], s_lg[:, :], axis=AX.X))
                vstep(ve.tensor_scalar(s_ra[:, :], s_lg[:, :], s_rm[:, 0:1], 0.0,
                                       ALU.subtract, ALU.is_ge))
                vstep(ve.tensor_scalar(s_rb[:, :], s_ra[:, :], -1e30, None, ALU.mult))
                vstep(ve.tensor_add(s_rb[:, :], s_lg[:, :], s_rb[:, :]))
                vstep(ve.reduce_max(s_rm[:, 1:2], s_rb[:, :], axis=AX.X))
                vstep(ve.tensor_scalar(s_rb[:, :], s_rb[:, :], s_rm[:, 1:2], 0.0,
                                       ALU.subtract, ALU.is_ge))
                vstep(ve.tensor_sub(s_rm[:, 2:3], s_rm[:, 1:2], s_rm[:, 0:1]))
                Eac.wait(sVE)
                astep(ac.activation(s_rm[:, 3:4], s_rm[:, 2:3], AF.Exp))
                Eve.wait(sAC)
                vstep(ve.tensor_scalar(s_rm[:, 4:5], s_rm[:, 3:4], 1.0, None,
                                       ALU.add))
                vstep(ve.reciprocal(s_rm[:, 5:6], s_rm[:, 4:5]))
                vstep(ve.tensor_mul(s_rm[:, 6:7], s_rm[:, 3:4], s_rm[:, 5:6]))
                vstep(ve.tensor_scalar(s_ra[:, :], s_ra[:, :], s_rm[:, 5:6], None,
                                       ALU.mult))
                vstep(ve.tensor_scalar(s_rb[:, :], s_rb[:, :], s_rm[:, 6:7], None,
                                       ALU.mult))
                vstep(ve.tensor_add(s_ra[:, :], s_ra[:, :], s_rb[:, :]))
                vstep(ve.tensor_mul(s_ra[:, :], s_ra[:, :], s_esel[:, :]))
                vstep(ve.reduce_sum(s_wcol[:, t:t + 1], s_ra[:, :], axis=AX.X))

            for fb in range(32):
                bank = PC if (fb % 2 == 0) else PD
                if len(phasea_evac) >= 2:
                    Epe.wait(sVE, phasea_evac[-2])
                for dblk in range(8):
                    mm = pe.matmul(bank[:, :],
                                   s_w1[:, dblk * 4096 + fb * 128:
                                        dblk * 4096 + (fb + 1) * 128],
                                   lnb[:, dblk * 512:(dblk + 1) * 512],
                                   start=(dblk == 0), stop=(dblk == 7))
                sPE.tick(mm)
                Eve.wait(sPE)
                vstep(ve.tensor_scalar(s_ht[:, fb * 512:(fb + 1) * 512], bank[:, :],
                                       s_b1t[:, fb:fb + 1], 0.0,
                                       ALU.add, ALU.max))
                phasea_evac.append(sVE.n)
            vfence()

            for t in range(4):
                P = PA if (t % 2 == 0) else PB
                Epe.wait(sVE, pa_free_ve[t % 2])
                Epe.wait(sVE)
                for fb in range(32):
                    lhs = s_ht[:, fb * 512 + t * 128: fb * 512 + (t + 1) * 128]
                    rhs = s_w2[:, fb * 1024:(fb + 1) * 1024]
                    pe.matmul(P[:, 0:512], lhs, rhs[:, 0:512],
                              start=(fb == 0), stop=(fb == 31))
                    mm = pe.matmul(P[:, 512:1024], lhs, rhs[:, 512:1024],
                                   start=(fb == 0), stop=(fb == 31))
                sPE.tick(mm)
                Eve.wait(sPE)
                Eve.wait(sC[(s // 4) % 2])
                Eve.wait(sC[(s // 4 + 1) % 2])
                vstep(ve.tensor_add(s_c1[:, :], P[:, :], s_b2b[:, :]))
                vstep(ve.tensor_scalar(s_c1[:, :], s_c1[:, :], s_wcol[:, t:t + 1],
                                       None, ALU.mult))
                vfence()
                pa_free_ve[t % 2] = sVE.n
                rowbase = (2 * s + t // 2) * 256 + (t % 2) * 128
                Esy.wait(sVE)
                sC[(s // 4) % 2].tick(
                    sy.dma_start(out=contrib.ap()[rowbase:rowbase + 128, :],
                                 in_=s_c1[:, :]), 16)
            ln1t_free[buf] = sPE.n

            if s % 4 == 3:
                cidx = s // 4
                Egp.wait(sC[cidx % 2])
                sCC.tick(gp.collective_compute(
                    "ReduceScatter", ALU.add, replica_groups=rg8,
                    ins=[contrib.ap()[cidx * 2048:(cidx + 1) * 2048, :].opt()],
                    outs=[rsout.ap()[cidx * 256:(cidx + 1) * 256, :].opt()]))
                rs_done.append(sCC.n)
                if cidx >= 1:
                    ffn_epilogue(cidx - 1)
        ffn_epilogue(7)
        Egp.wait(sST, 16 * 16)

    return nc


_NC_CACHE = []


def kernel(**inputs):
    def f(k):
        return np.ascontiguousarray(np.asarray(inputs[k], dtype=np.float32))

    x = f("L0_attn_x").reshape(NCORE * TOWN, D)
    wq, wk, wv, wo = f("L0_attn_Wq"), f("L0_attn_Wk"), f("L0_attn_Wv"), f("L0_attn_Wo")
    bq, bk, bv, bo = f("L0_attn_bq"), f("L0_attn_bk"), f("L0_attn_bv"), f("L0_attn_bo")
    wr, br = f("L0_ffn_Wr"), f("L0_ffn_br")
    w1, b1 = f("L0_ffn_W1"), f("L0_ffn_b1")
    w2, b2 = f("L0_ffn_W2"), f("L0_ffn_b2")
    gam, bet = f("L0_ffn_gamma"), f("L0_ffn_beta")

    wqT = np.ascontiguousarray(wq.T)
    wkT = np.ascontiguousarray(wk.T)
    wvT = np.ascontiguousarray(wv.T)
    woT = np.ascontiguousarray(wo.T)
    wrT = np.ascontiguousarray(wr.T)
    bqt = np.ascontiguousarray(bq.reshape(8, 128).T)
    bkt = np.ascontiguousarray(bk.reshape(8, 128).T)
    bvb = np.ascontiguousarray(np.broadcast_to(bv, (128, D)))
    bob = np.ascontiguousarray(np.broadcast_to(bo, (128, D)))
    gb = np.ascontiguousarray(np.broadcast_to(gam, (128, D)))
    beb = np.ascontiguousarray(np.broadcast_to(bet, (128, D)))
    brc = np.ascontiguousarray(br.reshape(8, 1))
    eye8 = np.eye(8, dtype=np.float32)

    if not _NC_CACHE:
        _NC_CACHE.append(build_nc())
    nc = _NC_CACHE[0]

    in_maps = []
    for c in range(NCORE):
        esel = np.zeros((128, 8), np.float32)
        esel[:, c] = 1.0
        in_maps.append(dict(
            x=np.ascontiguousarray(x[c * TOWN:(c + 1) * TOWN]),
            wqT=wqT, wkT=wkT, wvT=wvT, woT=woT,
            w1T=np.ascontiguousarray(w1[c].T),
            w2T=np.ascontiguousarray(w2[c].T),
            wrT=wrT, bqt=bqt, bkt=bkt,
            b1t=np.ascontiguousarray(b1[c].reshape(32, 128).T),
            brc=brc, bvb=bvb, bob=bob, gb=gb, beb=beb,
            b2b=np.ascontiguousarray(np.broadcast_to(b2[c], (128, D))),
            eye8=eye8, esel=esel,
        ))
    res = run_bass_kernel_spmd(nc, in_maps, core_ids=list(range(NCORE)))
    out = np.concatenate([np.asarray(res.results[c]["out"]) for c in range(NCORE)],
                         axis=0)
    return out.reshape(4, SEQ, D).astype(np.float32)


# version 08
# speedup vs baseline: 1.4772x; 1.4772x over previous
import numpy as np
import ml_dtypes
import contextlib
import concourse.bass as bass
import concourse.mybir as mybir
from concourse.bass_utils import run_bass_kernel_spmd
from concourse.replica_groups import maybe_share_collective_output_space

F32 = mybir.dt.float32
BF16 = mybir.dt.bfloat16
AF = mybir.ActivationFunctionType
ALU = mybir.AluOpType
AX = mybir.AxisListType
bfloat16 = ml_dtypes.bfloat16

NCORE = 8
D = 1024
SEQ = 4096
TOWN = 2048
HD = 64
NPAIR = 8
FF = 4096
SCALE = 1.0 / 32.0
CSHIFT = 4.0
EPS = 1e-5
NT_OWN = TOWN // 128
NMT = SEQ // 128
NSUB = 32
VSTRIDE = 1056
PBLOCKS = [(g, r) for g in range(8) for r in range(8)]


class Sem:
    def __init__(self, h):
        self.h = h
        self.n = 0

    def tick(self, instr, k=1):
        instr.then_inc(self.h, k)
        self.n += k
        return self.n


class FutI:

    def __init__(self):
        self.incs = []

    def then_inc(self, sem, k=1):
        self.incs.append((sem, k))
        return self


class Px:

    def __init__(self, block, name, nc):
        object.__setattr__(self, "_b", block)
        object.__setattr__(self, "_n", name)
        object.__setattr__(self, "_nc", nc)
        object.__setattr__(self, "_buf", [])

    def __getattr__(self, attr):
        def emit(*a, **k):
            fut = FutI()
            self._buf.append((attr, a, k, fut))
            return fut

        return emit

    def flush(self):
        buf = self._buf
        nc = self._nc

        def body(eng):
            with nc.allow_low_precision(
                    reason="deferred replay; individual low-precision sites "
                           "(softmax 1/sum bf16) are deliberate"):
                for attr, a, k, fut in buf:
                    instr = getattr(eng, attr)(*a, **k)
                    for sem, kk in fut.incs:
                        instr.then_inc(sem, kk)

        getattr(self._b, self._n)(body)


class EQ:
    def __init__(self, eng):
        self.e = eng
        self.waited = {}

    def wait(self, sem, val=None):
        v = sem.n if val is None else val
        if v <= 0 or self.waited.get(id(sem), -1) >= v:
            return
        self.e.wait_ge(sem.h, v)
        self.waited[id(sem)] = v


def build_nc():
    nc = bass.Bass("TRN2", target_bir_lowering=False, debug=False)

    x_d = nc.dram_tensor("x", [TOWN, D], F32, kind="ExternalInput").ap()
    wqT = nc.dram_tensor("wqT", [D, D], F32, kind="ExternalInput").ap()
    wkT = nc.dram_tensor("wkT", [D, D], F32, kind="ExternalInput").ap()
    wvT = nc.dram_tensor("wvT", [D, D], F32, kind="ExternalInput").ap()
    woT = nc.dram_tensor("woT", [D, D], F32, kind="ExternalInput").ap()
    w1T_d = nc.dram_tensor("w1T", [D, FF], F32, kind="ExternalInput").ap()
    w2T_d = nc.dram_tensor("w2T", [FF, D], F32, kind="ExternalInput").ap()
    wrT_d = nc.dram_tensor("wrT", [D, 8], F32, kind="ExternalInput").ap()
    bqt_d = nc.dram_tensor("bqt", [128, 8], F32, kind="ExternalInput").ap()
    bkt_d = nc.dram_tensor("bkt", [128, 8], F32, kind="ExternalInput").ap()
    b1t_d = nc.dram_tensor("b1t", [128, 32], F32, kind="ExternalInput").ap()
    brc_d = nc.dram_tensor("brc", [8, 1], F32, kind="ExternalInput").ap()
    bvb_d = nc.dram_tensor("bvb", [128, D], F32, kind="ExternalInput").ap()
    bob_d = nc.dram_tensor("bob", [128, D], F32, kind="ExternalInput").ap()
    gb_d = nc.dram_tensor("gb", [128, D], F32, kind="ExternalInput").ap()
    beb_d = nc.dram_tensor("beb", [128, D], F32, kind="ExternalInput").ap()
    b2b_d = nc.dram_tensor("b2b", [128, D], F32, kind="ExternalInput").ap()
    eye8_d = nc.dram_tensor("eye8", [8, 8], F32, kind="ExternalInput").ap()
    esel_d = nc.dram_tensor("esel", [128, 8], F32, kind="ExternalInput").ap()
    out_d = nc.dram_tensor("out", [TOWN, D], F32, kind="ExternalOutput").ap()

    rg8 = [list(range(NCORE))]
    rg2 = [[2 * b, 2 * b + 1] for b in range(4)]
    sp_ag8 = maybe_share_collective_output_space("AllGather", rg8)
    sp_ag2 = maybe_share_collective_output_space("AllGather", rg2)
    sp_rs = maybe_share_collective_output_space("ReduceScatter", rg8)
    ln0_loc = nc.dram_tensor("ln0_loc", [TOWN, D], BF16)
    ln0_pr = nc.dram_tensor("ln0_pr", [SEQ, D], BF16, addr_space=sp_ag2)
    ln1_loc = nc.dram_tensor("ln1_loc", [TOWN, D], BF16)
    ln1_ag = nc.dram_tensor("ln1_ag", [NCORE * TOWN, D], BF16, addr_space=sp_ag8)
    x1sp = nc.dram_tensor("x1sp", [TOWN, D], F32)
    contrib = nc.dram_tensor("contrib", [NCORE * TOWN, D], F32)
    rsout = nc.dram_tensor("rsout", [TOWN, D], F32, addr_space=sp_rs)
    bar_in = nc.dram_tensor("bar_in", [1, 64], F32)
    bar_out = nc.dram_tensor("bar_out", [1, 64], F32)

    SB0 = 16544
    R1 = 229344 - 67584
    R2 = R1 - 65536
    R3 = R2 - 32768
    R4 = R3 - 16384
    A = nc.alloc_sbuf_tensor_at
    s_vaug = A("s_vaug", [128, 33792], BF16, offset=R1)
    s_w1 = A("s_w1", [128, 32768], BF16, offset=R1)
    s_qt = A("s_qt", [128, 8192], BF16, offset=R2)
    s_kt = A("s_kt", [128, 4096], BF16, offset=R2 + 16384)
    s_ln0t = A("s_ln0t", [128, 8192], BF16, offset=R2 + 24576)
    s_equad = A("s_equad", [128, 8192], BF16, offset=R2 + 40960)
    s_bvb = A("s_bvb", [128, D], F32, offset=R2 + 57344)
    s_bob = A("s_bob", [128, D], F32, offset=R2 + 61440)
    s_w2 = A("s_w2", [128, 32768], BF16, offset=R2)
    s_attnT = A("s_attnT", [128, 16384], BF16, offset=R3)
    s_ht = A("s_ht", [128, 16384], BF16, offset=R3)
    s_wv = A("s_wv", [128, 8192], BF16, offset=R4)
    s_wqk = A("s_wqk", [128, 2048], BF16, offset=R4)
    s_wo = A("s_wo", [128, 8192], BF16, offset=R4)
    s_ln1t = A("s_ln1t", [128, 8192], BF16, offset=R4)

    off = [16544]

    def AL(name, shape, dtype):
        per = int(np.prod(shape[1:])) * (4 if dtype == F32 else 2)
        t = A(name, shape, dtype, offset=off[0])
        off[0] += (per + 31) // 32 * 32
        return t

    s_xt = AL("s_xt", [128, D], F32)
    s_s1 = AL("s_s1", [128, D], F32)
    s_s2 = AL("s_s2", [128, D], F32)
    s_ln1s = AL("s_ln1s", [128, D], BF16)
    s_gb = AL("s_gb", [128, D], BF16)
    s_beb = AL("s_beb", [128, D], BF16)
    s_b2b = AL("s_b2b", [128, D], BF16)
    s_c1 = AL("s_c1", [128, D], F32)
    s_r = AL("s_r", [1, 512], BF16)
    s_st = AL("s_st", [128, 32], F32)
    s_sc = AL("s_sc", [128, 8], F32)
    s_lgT = AL("s_lgT", [8, 512], F32)
    s_lg = AL("s_lg", [128, 8], F32)
    s_ra = AL("s_ra", [128, 8], F32)
    s_rb = AL("s_rb", [128, 8], F32)
    s_rm = AL("s_rm", [128, 8], F32)
    s_wcol = AL("s_wcol", [128, 4], F32)
    s_esel = AL("s_esel", [128, 8], F32)
    s_bq = AL("s_bq", [128, 8], F32)
    s_bk = AL("s_bk", [128, 8], F32)
    s_b1t = AL("s_b1t", [128, 32], F32)
    s_br = AL("s_br", [8, 1], F32)
    s_eye = AL("s_eye", [8, 8], F32)
    s_ones64 = AL("s_ones64", [1, 64], BF16)
    s_eps = AL("s_eps", [128, 1], F32)
    s_csh = AL("s_csh", [128, 1], F32)
    s_wr = AL("s_wr", [128, 64], BF16)
    assert off[0] <= R4, f"misc overflow {off[0]} > {R4}"

    PA = nc.alloc_psum_tensor("PA", [128, 1024], F32)
    PB = nc.alloc_psum_tensor("PB", [128, 1024], F32)
    PC = nc.alloc_psum_tensor("PC", [128, 512], F32)
    PD = nc.alloc_psum_tensor("PD", [128, 512], F32)
    PE_ = nc.alloc_psum_tensor("PEb", [128, 512], F32)
    PF = nc.alloc_psum_tensor("PF", [128, 512], F32)
    SROT = [PC, PD, PE_, PF]

    es = contextlib.ExitStack()
    sQ = Sem(es.enter_context(nc.semaphore("sQ")))
    sA = Sem(es.enter_context(nc.semaphore("sA")))
    sG = Sem(es.enter_context(nc.semaphore("sG")))
    sCT = Sem(es.enter_context(nc.semaphore("sCT")))
    sST = Sem(es.enter_context(nc.semaphore("sST")))
    sCC = Sem(es.enter_context(nc.semaphore("sCC")))
    sPE = Sem(es.enter_context(nc.semaphore("sPE")))
    sAC = Sem(es.enter_context(nc.semaphore("sAC")))
    sVE = Sem(es.enter_context(nc.semaphore("sVE")))
    sT0 = Sem(es.enter_context(nc.semaphore("sT0")))
    sT1 = Sem(es.enter_context(nc.semaphore("sT1")))
    sW = Sem(es.enter_context(nc.semaphore("sW")))
    sC0 = Sem(es.enter_context(nc.semaphore("sC0")))
    sC1 = Sem(es.enter_context(nc.semaphore("sC1")))
    sT = [sT0, sT1]
    sC = [sC0, sC1]
    block = es.enter_context(nc.Block())

    with es:
        sy = Px(block, "sync", nc)
        gp = Px(block, "gpsimd", nc)
        pe = Px(block, "tensor", nc)
        ac = Px(block, "scalar", nc)
        ve = Px(block, "vector", nc)
        Esy, Egp, Epe, Eac, Eve = EQ(sy), EQ(gp), EQ(pe), EQ(ac), EQ(ve)

        def vstep(instr):
            sVE.tick(instr)
            Eve.wait(sVE)

        def astep(instr):
            sAC.tick(instr)
            Eac.wait(sAC)

        def ln_tile(x_ap, scr_ap, out_ap, negmu_ap, rstd_ap):
            vstep(ve.reduce_sum(s_sc[:, 0:1], x_ap, axis=AX.X))
            vstep(ve.tensor_scalar(negmu_ap, s_sc[:, 0:1], -1.0 / D, None, ALU.mult))
            vstep(ve.tensor_scalar(scr_ap, x_ap, negmu_ap, None, ALU.add))
            vstep(ve.scalar_tensor_tensor(s_c1[:, :], scr_ap, 1.0, scr_ap,
                                          ALU.mult, ALU.mult,
                                          accum_out=s_sc[:, 1:2]))
            vstep(ve.tensor_scalar(s_sc[:, 1:2], s_sc[:, 1:2], EPS * D, None,
                                   ALU.add))
            Eac.wait(sVE)
            astep(ac.activation(s_sc[:, 2:3], s_sc[:, 1:2], AF.Ln, bias=0.0,
                                scale=1.0 / D))
            astep(ac.activation(rstd_ap, s_sc[:, 2:3], AF.Exp, bias=0.0, scale=-0.5))
            Eve.wait(sAC)
            vstep(ve.tensor_scalar(out_ap, scr_ap, rstd_ap, None, ALU.mult))

        def afence():
            astep(ac.activation(s_sc[:, 6:7], s_sc[:, 6:7], AF.Exp, scale=0.0))

        def vfence():
            vstep(ve.tensor_copy(s_sc[:, 7:8], s_sc[:, 7:8]))

        vstep(ve.memset(s_ones64[:, :], 1.0))
        vstep(ve.memset(s_sc[:, :], 0.0))
        vstep(ve.memset(s_st[:, :], 0.0))
        vstep(ve.memset(s_eps[:, :], EPS))
        vstep(ve.memset(s_csh[:, :], -CSHIFT))
        for _mi in range(4):
            vstep(ve.memset(s_vaug[:, _mi * 8448:(_mi + 1) * 8448], 1.0))
        for dst, src in [(s_bq, bqt_d), (s_bk, bkt_d), (s_b1t, b1t_d), (s_br, brc_d),
                         (s_eye, eye8_d), (s_bvb, bvb_d), (s_bob, bob_d),
                         (s_esel, esel_d)]:
            sQ.tick(sy.dma_start(out=dst[:, :], in_=src), 16)
        for dst, src in [(s_gb, gb_d), (s_beb, beb_d), (s_b2b, b2b_d)]:
            sG.tick(gp.dma_start(out=dst[:, :], in_=src), 16)
        for c in range(8):
            sG.tick(gp.dma_start(out=s_wv[:, c * 1024:(c + 1) * 1024],
                                 in_=wvT[c * 128:(c + 1) * 128, :]), 16)
        for dblk in range(8):
            sG.tick(gp.dma_start(out=s_wr[:, dblk * 8:(dblk + 1) * 8],
                                 in_=wrT_d[dblk * 128:(dblk + 1) * 128, :]), 16)

        for i in range(NT_OWN):
            Esy.wait(sVE, sVE.n)
            sQ.tick(sy.dma_start(out=s_xt[:, :], in_=x_d[i * 128:(i + 1) * 128, :]), 16)
            Eve.wait(sQ)
            Eac.wait(sQ)
            ln_tile(s_xt[:, :], s_s1[:, :], s_ln1s[:, :], s_sc[:, 4:5], s_sc[:, 5:6])
            vfence()
            Esy.wait(sVE)
            sQ.tick(sy.dma_start(out=ln0_loc.ap()[i * 128:(i + 1) * 128, :],
                                 in_=s_ln1s[:, :]), 16)
        sG.tick(gp.dma_start(out=bar_in.ap(), in_=x_d[0:1, 0:64]), 16)
        Egp.wait(sQ)
        Egp.wait(sG)
        sCC.tick(gp.collective_compute(
            "AllGather", ALU.bypass, replica_groups=rg2,
            ins=[ln0_loc.ap().opt()], outs=[ln0_pr.ap().opt()]))
        Egp.wait(sCC)
        sCC.tick(gp.collective_compute(
            "AllReduce", ALU.add, replica_groups=rg8,
            ins=[bar_in.ap().opt()], outs=[bar_out.ap().opt()]))
        CC_AG0 = sCC.n

        ln0t_free = [0, 0]

        def load_ln0t(mc, buf, rows_src):
            Esy.wait(sPE, ln0t_free[buf])
            Eac.wait(sPE, ln0t_free[buf])
            base = buf * 4096
            for c in range(8):
                eng = sy
                sT[buf].tick(eng.dma_start(
                    out=s_ln0t[:, base + c * 512: base + (c + 1) * 512],
                    in_=rows_src[mc * 512:(mc + 1) * 512, c * 128:(c + 1) * 128],
                    transpose=True), 16)

        Esy.wait(sCC, CC_AG0)
        Eac.wait(sCC, CC_AG0)
        vp_evac = []
        for mc in range(8):
            buf = mc % 2
            load_ln0t(mc, buf, ln0_pr.ap())
            Epe.wait(sT[buf])
            Epe.wait(sG)
            for mt in range(4):
                m = mc * 4 + mt
                P = PA if (m % 2 == 0) else PB
                if m >= 2:
                    Epe.wait(sVE, vp_evac[m - 2])
                for c in range(8):
                    lhs = s_ln0t[:, buf * 4096 + c * 512 + mt * 128:
                                 buf * 4096 + c * 512 + (mt + 1) * 128]
                    rhs = s_wv[:, c * 1024:(c + 1) * 1024]
                    pe.matmul(P[:, 0:512], lhs, rhs[:, 0:512],
                              start=(c == 0), stop=(c == 7))
                    last = pe.matmul(P[:, 512:1024], lhs, rhs[:, 512:1024],
                                     start=(c == 0), stop=(c == 7))
                sPE.tick(last)
                Eve.wait(sPE)
                vstep(ve.tensor_add(s_s1[:, :], P[:, :], s_bvb[:, :]))
                for hh in range(16):
                    cp = ve.tensor_copy(
                        s_vaug[:, m * VSTRIDE + hh * 66: m * VSTRIDE + hh * 66 + 64],
                        s_s1[:, hh * 64:(hh + 1) * 64])
                sVE.tick(cp)
                Eve.wait(sVE)
                vfence()
                vp_evac.append(sVE.n)
            ln0t_free[buf] = sPE.n
        V_READY_VE = sVE.n

        qk_free_pe = [sPE.n]
        qk_evac = []

        def qk_proj_pair(p):
            Epe.wait(sVE, V_READY_VE)
            Egp.wait(sPE, qk_free_pe[0])
            for c in range(8):
                sG.tick(gp.dma_start(out=s_wqk[:, c * 128:(c + 1) * 128],
                                     in_=wqT[c * 128:(c + 1) * 128,
                                             p * 128:(p + 1) * 128]), 16)
                sG.tick(gp.dma_start(out=s_wqk[:, 1024 + c * 128:1024 + (c + 1) * 128],
                                     in_=wkT[c * 128:(c + 1) * 128,
                                             p * 128:(p + 1) * 128]), 16)
            q = p % 2
            for j in range(12):
                is_q = j < 8
                mc = j if is_q else j - 8
                buf = j % 2
                load_ln0t(mc, buf, ln0_pr.ap() if is_q else ln0_loc.ap())
                Epe.wait(sT[buf])
                Epe.wait(sG)
                P = PA if (j % 2 == 0) else PB
                if len(qk_evac) >= 2:
                    Epe.wait(sVE, qk_evac[-2])
                woff = 0 if is_q else 1024
                for c in range(8):
                    last = pe.matmul(
                        P[:, 0:512],
                        s_wqk[:, woff + c * 128: woff + (c + 1) * 128],
                        s_ln0t[:, buf * 4096 + c * 512: buf * 4096 + (c + 1) * 512],
                        start=(c == 0), stop=(c == 7))
                sPE.tick(last)
                Eve.wait(sPE)
                if is_q:
                    dst = s_qt[:, q * 4096 + mc * 512: q * 4096 + (mc + 1) * 512]
                    bias = s_bq[:, p:p + 1]
                else:
                    dst = s_kt[:, q * 2048 + mc * 512: q * 2048 + (mc + 1) * 512]
                    bias = s_bk[:, p:p + 1]
                vstep(ve.tensor_scalar(dst, P[:, 0:512], bias, None, ALU.add))
                qk_evac.append(sVE.n)
                ln0t_free[buf] = sPE.n
            vfence()
            qk_free_pe[0] = sPE.n

        ATTN_VE = [0]

        def attn_pair(p):
            q = p % 2
            Epe.wait(sVE)
            Epe.wait(sVE, V_READY_VE)
            for nch in range(4):
                nbase = nch * 512
                Epe.wait(sVE)
                s_done = []
                for quad in range(4):
                    for mi in range(8):
                        m = quad * 8 + mi
                        for hh in range(2):
                            k = mi * 2 + hh
                            n_op = quad * 16 + k
                            bank = SROT[n_op % 4]
                            if n_op >= 4:
                                Epe.wait(sAC, s_done[n_op - 4])
                            lhs = s_qt[hh * 64:(hh + 1) * 64,
                                       q * 4096 + m * 128: q * 4096 + (m + 1) * 128]
                            rhs = s_kt[hh * 64:(hh + 1) * 64,
                                       q * 2048 + nbase: q * 2048 + nbase + 512]
                            mm = pe.matmul(bank[:, :], lhs, rhs, start=True,
                                           stop=True, tile_position=(hh * 64, 0))
                            sPE.tick(mm)
                            Eac.wait(sPE)
                            ex = ac.activation(s_equad[:, k * 512:(k + 1) * 512],
                                               bank[:, :], AF.Exp,
                                               bias=0.0, scale=SCALE)
                            s_done.append(sAC.tick(ex))
                    afence()
                    fq = sAC.n
                    for mi in range(8):
                        m = quad * 8 + mi
                        for hh in range(2):
                            k = mi * 2 + hh
                            Epe.wait(sAC, fq)
                            lhs = s_vaug[:, m * VSTRIDE + (p * 2 + hh) * 66:
                                         m * VSTRIDE + (p * 2 + hh) * 66 + 65]
                            mm = pe.matmul(PB[0:65, hh * 512:(hh + 1) * 512], lhs,
                                           s_equad[:, k * 512:(k + 1) * 512],
                                           start=(m == 0), stop=(m == NMT - 1))
                    sPE.tick(mm)
                for hh in range(2):
                    Eve.wait(sPE)
                    vstep(ve.tensor_copy(s_s2[0:65, 0:512],
                                         PB[0:65, hh * 512:(hh + 1) * 512]))
                    with nc.allow_low_precision(
                            reason="softmax 1/sum stored bf16 for the rank-1 "
                                   "broadcast matmul; same error class as bf16 "
                                   "matmul operands"):
                        vstep(ve.reciprocal(s_r[:, :], s_s2[64:65, 0:512]))
                    vfence()
                    Epe.wait(sVE)
                    mm = pe.matmul(PC[0:64, :], s_ones64[:, :], s_r[:, :],
                                   start=True, stop=True)
                    sPE.tick(mm)
                    Eve.wait(sPE)
                    dst = s_attnT[hh * 64:(hh + 1) * 64,
                                  p * 2048 + nbase: p * 2048 + nbase + 512]
                    vstep(ve.tensor_mul(dst, s_s2[0:64, 0:512], PC[0:64, :]))
            vfence()
            ATTN_VE[0] = sVE.n

        for p in range(NPAIR):
            qk_proj_pair(p)
            attn_pair(p)
        QT_DEAD_PE = sPE.n

        Egp.wait(sPE, qk_free_pe[0])
        for c in range(8):
            sG.tick(gp.dma_start(out=s_wo[:, c * 1024:(c + 1) * 1024],
                                 in_=woT[c * 128:(c + 1) * 128, :]), 16)
        WO_G = sG.n
        Egp.wait(sPE, QT_DEAD_PE)
        for dblk in range(8):
            sW.tick(gp.dma_start(out=s_w1[:, dblk * 4096:(dblk + 1) * 4096],
                                 in_=w1T_d[dblk * 128:(dblk + 1) * 128, :]), 16)
        for fb in range(30):
            sW.tick(gp.dma_start(out=s_w2[:, fb * 1024:(fb + 1) * 1024],
                                 in_=w2T_d[fb * 128:(fb + 1) * 128, :]), 16)

        Epe.wait(sVE, ATTN_VE[0])
        Epe.wait(sG, WO_G)
        pa_ok = 0
        for i in range(NT_OWN):
            Epe.wait(sVE, pa_ok)
            for fb in range(8):
                lhs = s_attnT[:, fb * 2048 + i * 128: fb * 2048 + (i + 1) * 128]
                rhs = s_wo[:, fb * 1024:(fb + 1) * 1024]
                pe.matmul(PA[:, 0:512], lhs, rhs[:, 0:512],
                          start=(fb == 0), stop=(fb == 7))
                mm = pe.matmul(PA[:, 512:1024], lhs, rhs[:, 512:1024],
                               start=(fb == 0), stop=(fb == 7))
            sPE.tick(mm)
            Esy.wait(sVE)
            sQ.tick(sy.dma_start(out=s_xt[:, :], in_=x_d[i * 128:(i + 1) * 128, :]), 16)
            Eve.wait(sPE)
            Eve.wait(sQ)
            vstep(ve.tensor_add(s_s1[:, :], PA[:, :], s_bob[:, :]))
            pa_ok = sVE.n
            ln_tile(s_s1[:, :], s_s2[:, :], s_s2[:, :], s_sc[:, 4:5], s_sc[:, 5:6])
            Eve.wait(sQ)
            vstep(ve.tensor_add(s_s1[:, :], s_xt[:, :], s_s2[:, :]))
            vfence()
            Esy.wait(sVE)
            sQ.tick(sy.dma_start(out=x1sp.ap()[i * 128:(i + 1) * 128, :],
                                 in_=s_s1[:, :]), 16)
            ln_tile(s_s1[:, :], s_s2[:, :], s_ln1s[:, :],
                    s_st[:, 2 * i:2 * i + 1], s_st[:, 2 * i + 1:2 * i + 2])
            vfence()
            Esy.wait(sVE)
            sQ.tick(sy.dma_start(out=ln1_loc.ap()[i * 128:(i + 1) * 128, :],
                                 in_=s_ln1s[:, :]), 16)
        OUTPROJ_PE = sPE.n
        Egp.wait(sVE)
        for fb in (30, 31):
            sW.tick(gp.dma_start(out=s_w2[:, fb * 1024:(fb + 1) * 1024],
                                 in_=w2T_d[fb * 128:(fb + 1) * 128, :]), 16)
        W12_W = sW.n
        Egp.wait(sQ)
        sCC.tick(gp.collective_compute(
            "AllGather", ALU.bypass, replica_groups=rg8,
            ins=[ln1_loc.ap().opt()], outs=[ln1_ag.ap().opt()]))
        Egp.wait(sCC)
        sCC.tick(gp.collective_compute(
            "AllReduce", ALU.add, replica_groups=rg8,
            ins=[bar_in.ap().opt()], outs=[bar_out.ap().opt()]))
        CC_AG1 = sCC.n

        ln1t_free = [OUTPROJ_PE, OUTPROJ_PE]
        pa_free_ve = [0, 0]
        rs_done = []
        phasea_evac = []

        def load_ln1t(s, buf):
            Esy.wait(sPE, ln1t_free[buf])
            Eac.wait(sPE, ln1t_free[buf])
            for bi in range(2):
                g, r = PBLOCKS[2 * s + bi]
                rows = ln1_ag.ap()[r * TOWN + g * 256: r * TOWN + (g + 1) * 256, :]
                for c in range(8):
                    eng = sy
                    sT[buf].tick(eng.dma_start(
                        out=s_ln1t[:, buf * 4096 + c * 512 + bi * 256:
                                   buf * 4096 + c * 512 + bi * 256 + 256],
                        in_=rows[:, c * 128:(c + 1) * 128], transpose=True), 16)

        def ffn_epilogue(cidx):
            Esy.wait(sCC, rs_done[cidx])
            Eve.wait(sC[0])
            Eve.wait(sC[1])
            for k in range(2):
                i = cidx * 2 + k
                Esy.wait(sVE)
                sQ.tick(sy.dma_start(out=s_xt[:, :],
                                     in_=x1sp.ap()[i * 128:(i + 1) * 128, :]), 16)
                sQ.tick(sy.dma_start(out=s_s2[:, :],
                                     in_=rsout.ap()[i * 128:(i + 1) * 128, :]), 16)
                Eve.wait(sQ)
                Eve.wait(sST)
                vstep(ve.tensor_scalar(s_s1[:, :], s_xt[:, :],
                                       s_st[:, 2 * i:2 * i + 1],
                                       s_st[:, 2 * i + 1:2 * i + 2],
                                       ALU.add, ALU.mult))
                vstep(ve.tensor_add(s_s1[:, :], s_s1[:, :], s_s2[:, :]))
                ln_tile(s_s1[:, :], s_s2[:, :], s_s2[:, :], s_sc[:, 4:5],
                        s_sc[:, 5:6])
                vstep(ve.tensor_mul(s_s2[:, :], s_s2[:, :], s_gb[:, :]))
                vstep(ve.tensor_add(s_s2[:, :], s_s2[:, :], s_beb[:, :]))
                ln_tile(s_s2[:, :], s_s1[:, :], s_s1[:, :], s_sc[:, 4:5],
                        s_sc[:, 5:6])
                vstep(ve.tensor_add(s_s1[:, :], s_s1[:, :], s_xt[:, :]))
                vfence()
                Eac.wait(sVE)
                sST.tick(ac.dma_start(out=out_d[i * 128:(i + 1) * 128, :],
                                      in_=s_s1[:, :]), 16)

        Esy.wait(sCC, CC_AG1)
        Eac.wait(sCC, CC_AG1)
        Epe.wait(sW, W12_W)
        Epe.wait(sVE)
        Epe.wait(sAC)

        for s in range(NSUB):
            buf = s % 2
            load_ln1t(s, buf)
            Epe.wait(sT[buf])
            lnb = s_ln1t[:, buf * 4096:(buf + 1) * 4096]

            Epe.wait(sAC)
            for dblk in range(8):
                mm = pe.matmul(PE_[0:8, :], s_wr[:, dblk * 8:(dblk + 1) * 8],
                               lnb[:, dblk * 512:(dblk + 1) * 512],
                               start=(dblk == 0), stop=(dblk == 7))
            sPE.tick(mm)
            Eve.wait(sPE)
            vstep(ve.tensor_scalar(s_lgT[:, :], PE_[0:8, :], s_br[:, :], None,
                                   ALU.add))
            vfence()
            for t in range(4):
                Epe.wait(sVE)
                mm = pe.transpose(PF[:, 0:8], s_lgT[:, t * 128:(t + 1) * 128],
                                  s_eye[:, :])
                sPE.tick(mm)
                Eve.wait(sPE)
                vstep(ve.tensor_copy(s_lg[:, :], PF[:, 0:8]))
                vstep(ve.reduce_max(s_rm[:, 0:1], s_lg[:, :], axis=AX.X))
                vstep(ve.tensor_scalar(s_ra[:, :], s_lg[:, :], s_rm[:, 0:1], 0.0,
                                       ALU.subtract, ALU.is_ge))
                vstep(ve.tensor_scalar(s_rb[:, :], s_ra[:, :], -1e30, None, ALU.mult))
                vstep(ve.tensor_add(s_rb[:, :], s_lg[:, :], s_rb[:, :]))
                vstep(ve.reduce_max(s_rm[:, 1:2], s_rb[:, :], axis=AX.X))
                vstep(ve.tensor_scalar(s_rb[:, :], s_rb[:, :], s_rm[:, 1:2], 0.0,
                                       ALU.subtract, ALU.is_ge))
                vstep(ve.tensor_sub(s_rm[:, 2:3], s_rm[:, 1:2], s_rm[:, 0:1]))
                Eac.wait(sVE)
                astep(ac.activation(s_rm[:, 3:4], s_rm[:, 2:3], AF.Exp))
                Eve.wait(sAC)
                vstep(ve.tensor_scalar(s_rm[:, 4:5], s_rm[:, 3:4], 1.0, None,
                                       ALU.add))
                vstep(ve.reciprocal(s_rm[:, 5:6], s_rm[:, 4:5]))
                vstep(ve.tensor_mul(s_rm[:, 6:7], s_rm[:, 3:4], s_rm[:, 5:6]))
                vstep(ve.tensor_scalar(s_ra[:, :], s_ra[:, :], s_rm[:, 5:6], None,
                                       ALU.mult))
                vstep(ve.tensor_scalar(s_rb[:, :], s_rb[:, :], s_rm[:, 6:7], None,
                                       ALU.mult))
                vstep(ve.tensor_add(s_ra[:, :], s_ra[:, :], s_rb[:, :]))
                vstep(ve.tensor_mul(s_ra[:, :], s_ra[:, :], s_esel[:, :]))
                vstep(ve.reduce_sum(s_wcol[:, t:t + 1], s_ra[:, :], axis=AX.X))

            for fb in range(32):
                bank = PC if (fb % 2 == 0) else PD
                if len(phasea_evac) >= 2:
                    Epe.wait(sVE, phasea_evac[-2])
                for dblk in range(8):
                    mm = pe.matmul(bank[:, :],
                                   s_w1[:, dblk * 4096 + fb * 128:
                                        dblk * 4096 + (fb + 1) * 128],
                                   lnb[:, dblk * 512:(dblk + 1) * 512],
                                   start=(dblk == 0), stop=(dblk == 7))
                sPE.tick(mm)
                Eve.wait(sPE)
                vstep(ve.tensor_scalar(s_ht[:, fb * 512:(fb + 1) * 512], bank[:, :],
                                       s_b1t[:, fb:fb + 1], 0.0,
                                       ALU.add, ALU.max))
                phasea_evac.append(sVE.n)
            vfence()

            for t in range(4):
                P = PA if (t % 2 == 0) else PB
                Epe.wait(sVE, pa_free_ve[t % 2])
                Epe.wait(sVE)
                for fb in range(32):
                    lhs = s_ht[:, fb * 512 + t * 128: fb * 512 + (t + 1) * 128]
                    rhs = s_w2[:, fb * 1024:(fb + 1) * 1024]
                    pe.matmul(P[:, 0:512], lhs, rhs[:, 0:512],
                              start=(fb == 0), stop=(fb == 31))
                    mm = pe.matmul(P[:, 512:1024], lhs, rhs[:, 512:1024],
                                   start=(fb == 0), stop=(fb == 31))
                sPE.tick(mm)
                Eve.wait(sPE)
                Eve.wait(sC[(s // 4) % 2])
                Eve.wait(sC[(s // 4 + 1) % 2])
                vstep(ve.tensor_add(s_c1[:, :], P[:, :], s_b2b[:, :]))
                vstep(ve.tensor_scalar(s_c1[:, :], s_c1[:, :], s_wcol[:, t:t + 1],
                                       None, ALU.mult))
                vfence()
                pa_free_ve[t % 2] = sVE.n
                rowbase = (2 * s + t // 2) * 256 + (t % 2) * 128
                Esy.wait(sVE)
                sC[(s // 4) % 2].tick(
                    sy.dma_start(out=contrib.ap()[rowbase:rowbase + 128, :],
                                 in_=s_c1[:, :]), 16)
            ln1t_free[buf] = sPE.n

            if s % 4 == 3:
                cidx = s // 4
                Egp.wait(sC[cidx % 2])
                sCC.tick(gp.collective_compute(
                    "ReduceScatter", ALU.add, replica_groups=rg8,
                    ins=[contrib.ap()[cidx * 2048:(cidx + 1) * 2048, :].opt()],
                    outs=[rsout.ap()[cidx * 256:(cidx + 1) * 256, :].opt()]))
                rs_done.append(sCC.n)
                if cidx >= 1:
                    ffn_epilogue(cidx - 1)
        ffn_epilogue(7)
        Egp.wait(sST, 16 * 16)
        for eng_px in (sy, gp, pe, ac, ve):
            eng_px.flush()

    return nc


_NC_CACHE = []


def kernel(**inputs):
    def f(k):
        return np.ascontiguousarray(np.asarray(inputs[k], dtype=np.float32))

    x = f("L0_attn_x").reshape(NCORE * TOWN, D)
    wq, wk, wv, wo = f("L0_attn_Wq"), f("L0_attn_Wk"), f("L0_attn_Wv"), f("L0_attn_Wo")
    bq, bk, bv, bo = f("L0_attn_bq"), f("L0_attn_bk"), f("L0_attn_bv"), f("L0_attn_bo")
    wr, br = f("L0_ffn_Wr"), f("L0_ffn_br")
    w1, b1 = f("L0_ffn_W1"), f("L0_ffn_b1")
    w2, b2 = f("L0_ffn_W2"), f("L0_ffn_b2")
    gam, bet = f("L0_ffn_gamma"), f("L0_ffn_beta")

    wqT = np.ascontiguousarray(wq.T)
    wkT = np.ascontiguousarray(wk.T)
    wvT = np.ascontiguousarray(wv.T)
    woT = np.ascontiguousarray(wo.T)
    wrT = np.ascontiguousarray(wr.T)
    bqt = np.ascontiguousarray(bq.reshape(8, 128).T)
    bkt = np.ascontiguousarray(bk.reshape(8, 128).T)
    bvb = np.ascontiguousarray(np.broadcast_to(bv, (128, D)))
    bob = np.ascontiguousarray(np.broadcast_to(bo, (128, D)))
    gb = np.ascontiguousarray(np.broadcast_to(gam, (128, D)))
    beb = np.ascontiguousarray(np.broadcast_to(bet, (128, D)))
    brc = np.ascontiguousarray(br.reshape(8, 1))
    eye8 = np.eye(8, dtype=np.float32)

    if not _NC_CACHE:
        _NC_CACHE.append(build_nc())
    nc = _NC_CACHE[0]

    in_maps = []
    for c in range(NCORE):
        esel = np.zeros((128, 8), np.float32)
        esel[:, c] = 1.0
        in_maps.append(dict(
            x=np.ascontiguousarray(x[c * TOWN:(c + 1) * TOWN]),
            wqT=wqT, wkT=wkT, wvT=wvT, woT=woT,
            w1T=np.ascontiguousarray(w1[c].T),
            w2T=np.ascontiguousarray(w2[c].T),
            wrT=wrT, bqt=bqt, bkt=bkt,
            b1t=np.ascontiguousarray(b1[c].reshape(32, 128).T),
            brc=brc, bvb=bvb, bob=bob, gb=gb, beb=beb,
            b2b=np.ascontiguousarray(np.broadcast_to(b2[c], (128, D))),
            eye8=eye8, esel=esel,
        ))
    res = run_bass_kernel_spmd(nc, in_maps, core_ids=list(range(NCORE)))
    out = np.concatenate([np.asarray(res.results[c]["out"]) for c in range(NCORE)],
                         axis=0)
    return out.reshape(4, SEQ, D).astype(np.float32)


# version 09 (final)
# speedup vs baseline: 1.5316x; 1.0368x over previous
import numpy as np
import ml_dtypes
import contextlib
import concourse.bass as bass
import concourse.mybir as mybir
from concourse.bass_utils import run_bass_kernel_spmd
from concourse.replica_groups import maybe_share_collective_output_space

F32 = mybir.dt.float32
BF16 = mybir.dt.bfloat16
AF = mybir.ActivationFunctionType
ALU = mybir.AluOpType
AX = mybir.AxisListType
bfloat16 = ml_dtypes.bfloat16

NCORE = 8
D = 1024
SEQ = 4096
TOWN = 2048
HD = 64
NPAIR = 8
FF = 4096
SCALE = 1.0 / 32.0
CSHIFT = 4.0
EPS = 1e-5
NT_OWN = TOWN // 128
NMT = SEQ // 128
NSUB = 32
VSTRIDE = 1056
PBLOCKS = [(g, r) for g in range(8) for r in range(8)]


class Sem:
    def __init__(self, h):
        self.h = h
        self.n = 0

    def tick(self, instr, k=1):
        instr.then_inc(self.h, k)
        self.n += k
        return self.n


class FutI:

    def __init__(self):
        self.incs = []

    def then_inc(self, sem, k=1):
        self.incs.append((sem, k))
        return self


class Px:

    def __init__(self, block, name, nc):
        object.__setattr__(self, "_b", block)
        object.__setattr__(self, "_n", name)
        object.__setattr__(self, "_nc", nc)
        object.__setattr__(self, "_buf", [])

    def __getattr__(self, attr):
        def emit(*a, **k):
            fut = FutI()
            self._buf.append((attr, a, k, fut))
            return fut

        return emit

    def flush(self):
        buf = self._buf
        nc = self._nc

        def body(eng):
            with nc.allow_low_precision(
                    reason="deferred replay; individual low-precision sites "
                           "(softmax 1/sum bf16) are deliberate"):
                for attr, a, k, fut in buf:
                    instr = getattr(eng, attr)(*a, **k)
                    for sem, kk in fut.incs:
                        instr.then_inc(sem, kk)

        getattr(self._b, self._n)(body)


class EQ:
    def __init__(self, eng):
        self.e = eng
        self.waited = {}

    def wait(self, sem, val=None):
        v = sem.n if val is None else val
        if v <= 0 or self.waited.get(id(sem), -1) >= v:
            return
        self.e.wait_ge(sem.h, v)
        self.waited[id(sem)] = v


def build_nc():
    nc = bass.Bass("TRN2", target_bir_lowering=False, debug=False)

    x_d = nc.dram_tensor("x", [TOWN, D], F32, kind="ExternalInput").ap()
    wqT = nc.dram_tensor("wqT", [D, D], F32, kind="ExternalInput").ap()
    wkT = nc.dram_tensor("wkT", [D, D], F32, kind="ExternalInput").ap()
    wvT = nc.dram_tensor("wvT", [D, D], F32, kind="ExternalInput").ap()
    woT = nc.dram_tensor("woT", [D, D], F32, kind="ExternalInput").ap()
    w1T_d = nc.dram_tensor("w1T", [D, FF], F32, kind="ExternalInput").ap()
    w2T_d = nc.dram_tensor("w2T", [FF, D], F32, kind="ExternalInput").ap()
    wrT_d = nc.dram_tensor("wrT", [D, 8], F32, kind="ExternalInput").ap()
    bqt_d = nc.dram_tensor("bqt", [128, 8], F32, kind="ExternalInput").ap()
    bkt_d = nc.dram_tensor("bkt", [128, 8], F32, kind="ExternalInput").ap()
    b1t_d = nc.dram_tensor("b1t", [128, 32], F32, kind="ExternalInput").ap()
    brc_d = nc.dram_tensor("brc", [8, 1], F32, kind="ExternalInput").ap()
    bvb_d = nc.dram_tensor("bvb", [128, D], F32, kind="ExternalInput").ap()
    bob_d = nc.dram_tensor("bob", [128, D], F32, kind="ExternalInput").ap()
    gb_d = nc.dram_tensor("gb", [128, D], F32, kind="ExternalInput").ap()
    beb_d = nc.dram_tensor("beb", [128, D], F32, kind="ExternalInput").ap()
    b2b_d = nc.dram_tensor("b2b", [128, D], F32, kind="ExternalInput").ap()
    eye8_d = nc.dram_tensor("eye8", [8, 8], F32, kind="ExternalInput").ap()
    esel_d = nc.dram_tensor("esel", [128, 8], F32, kind="ExternalInput").ap()
    out_d = nc.dram_tensor("out", [TOWN, D], F32, kind="ExternalOutput").ap()

    rg8 = [list(range(NCORE))]
    rg2 = [[2 * b, 2 * b + 1] for b in range(4)]
    sp_ag8 = maybe_share_collective_output_space("AllGather", rg8)
    sp_ag2 = maybe_share_collective_output_space("AllGather", rg2)
    sp_rs = maybe_share_collective_output_space("ReduceScatter", rg8)
    ln0_loc = nc.dram_tensor("ln0_loc", [TOWN, D], BF16)
    ln0_pr = nc.dram_tensor("ln0_pr", [SEQ, D], BF16, addr_space=sp_ag2)
    ln1_loc = nc.dram_tensor("ln1_loc", [TOWN, D], BF16)
    ln1_ag = nc.dram_tensor("ln1_ag", [NCORE * TOWN, D], BF16, addr_space=sp_ag8)
    x1sp = nc.dram_tensor("x1sp", [TOWN, D], F32)
    contrib = nc.dram_tensor("contrib", [NCORE * TOWN, D], F32)
    rsout = nc.dram_tensor("rsout", [TOWN, D], F32, addr_space=sp_rs)
    bar_in = nc.dram_tensor("bar_in", [1, 64], F32)
    bar_out = nc.dram_tensor("bar_out", [1, 64], F32)

    SB0 = 16544
    R1 = 229344 - 67584
    R2 = R1 - 65536
    R3 = R2 - 32768
    R4 = R3 - 16384
    A = nc.alloc_sbuf_tensor_at
    s_vaug = A("s_vaug", [128, 33792], BF16, offset=R1)
    s_w1 = A("s_w1", [128, 32768], BF16, offset=R1)
    s_qt = A("s_qt", [128, 8192], BF16, offset=R2)
    s_kt = A("s_kt", [128, 4096], BF16, offset=R2 + 16384)
    s_ln0t = A("s_ln0t", [128, 8192], BF16, offset=R2 + 24576)
    s_equad = A("s_equad", [128, 8192], BF16, offset=R2 + 40960)
    s_bvb = A("s_bvb", [128, D], F32, offset=R2 + 57344)
    s_bob = A("s_bob", [128, D], F32, offset=R2 + 61440)
    s_w2 = A("s_w2", [128, 32768], BF16, offset=R2)
    s_attnT = A("s_attnT", [128, 16384], BF16, offset=R3)
    s_ht = A("s_ht", [128, 16384], BF16, offset=R3)
    s_wv = A("s_wv", [128, 8192], BF16, offset=R4)
    s_wqk = A("s_wqk", [128, 2048], BF16, offset=R4)
    s_wo = A("s_wo", [128, 8192], BF16, offset=R4)
    s_ln1t = A("s_ln1t", [128, 8192], BF16, offset=R4)

    off = [16544]

    def AL(name, shape, dtype):
        per = int(np.prod(shape[1:])) * (4 if dtype == F32 else 2)
        t = A(name, shape, dtype, offset=off[0])
        off[0] += (per + 31) // 32 * 32
        return t

    s_xt = AL("s_xt", [128, D], F32)
    s_s1 = AL("s_s1", [128, D], F32)
    s_s2 = AL("s_s2", [128, D], F32)
    s_ln1s = AL("s_ln1s", [128, D], BF16)
    s_gb = AL("s_gb", [128, D], BF16)
    s_beb = AL("s_beb", [128, D], BF16)
    s_b2b = AL("s_b2b", [128, D], BF16)
    s_c1 = AL("s_c1", [128, D], F32)
    s_r = AL("s_r", [1, 512], BF16)
    s_st = AL("s_st", [128, 32], F32)
    s_sc = AL("s_sc", [128, 8], F32)
    s_lgT = AL("s_lgT", [8, 512], F32)
    s_lg = AL("s_lg", [128, 8], F32)
    s_ra = AL("s_ra", [128, 8], F32)
    s_rb = AL("s_rb", [128, 8], F32)
    s_rm = AL("s_rm", [128, 8], F32)
    s_wcol = AL("s_wcol", [128, 4], F32)
    s_esel = AL("s_esel", [128, 8], F32)
    s_bq = AL("s_bq", [128, 8], F32)
    s_bk = AL("s_bk", [128, 8], F32)
    s_b1t = AL("s_b1t", [128, 32], F32)
    s_br = AL("s_br", [8, 1], F32)
    s_eye = AL("s_eye", [8, 8], F32)
    s_ones64 = AL("s_ones64", [1, 64], BF16)
    s_eps = AL("s_eps", [128, 1], F32)
    s_csh = AL("s_csh", [128, 1], F32)
    s_wr = AL("s_wr", [128, 64], BF16)
    assert off[0] <= R4, f"misc overflow {off[0]} > {R4}"

    PA = nc.alloc_psum_tensor("PA", [128, 1024], F32)
    PB = nc.alloc_psum_tensor("PB", [128, 1024], F32)
    PC = nc.alloc_psum_tensor("PC", [128, 512], F32)
    PD = nc.alloc_psum_tensor("PD", [128, 512], F32)
    PE_ = nc.alloc_psum_tensor("PEb", [128, 512], F32)
    PF = nc.alloc_psum_tensor("PF", [128, 512], F32)
    SROT = [PC, PD, PE_, PF]

    es = contextlib.ExitStack()
    sQ = Sem(es.enter_context(nc.semaphore("sQ")))
    sA = Sem(es.enter_context(nc.semaphore("sA")))
    sG = Sem(es.enter_context(nc.semaphore("sG")))
    sCT = Sem(es.enter_context(nc.semaphore("sCT")))
    sST = Sem(es.enter_context(nc.semaphore("sST")))
    sCC = Sem(es.enter_context(nc.semaphore("sCC")))
    sPE = Sem(es.enter_context(nc.semaphore("sPE")))
    sAC = Sem(es.enter_context(nc.semaphore("sAC")))
    sVE = Sem(es.enter_context(nc.semaphore("sVE")))
    sT0 = Sem(es.enter_context(nc.semaphore("sT0")))
    sT1 = Sem(es.enter_context(nc.semaphore("sT1")))
    sW = Sem(es.enter_context(nc.semaphore("sW")))
    sC0 = Sem(es.enter_context(nc.semaphore("sC0")))
    sC1 = Sem(es.enter_context(nc.semaphore("sC1")))
    sT = [sT0, sT1]
    sC = [sC0, sC1]
    block = es.enter_context(nc.Block())

    with es:
        sy = Px(block, "sync", nc)
        gp = Px(block, "gpsimd", nc)
        pe = Px(block, "tensor", nc)
        ac = Px(block, "scalar", nc)
        ve = Px(block, "vector", nc)
        Esy, Egp, Epe, Eac, Eve = EQ(sy), EQ(gp), EQ(pe), EQ(ac), EQ(ve)

        def vstep(instr):
            sVE.tick(instr)
            Eve.wait(sVE)

        def astep(instr):
            sAC.tick(instr)
            Eac.wait(sAC)

        def ln_tile(x_ap, scr_ap, out_ap, negmu_ap, rstd_ap):
            vstep(ve.reduce_sum(s_sc[:, 0:1], x_ap, axis=AX.X))
            vstep(ve.tensor_scalar(negmu_ap, s_sc[:, 0:1], -1.0 / D, None, ALU.mult))
            vstep(ve.tensor_scalar(scr_ap, x_ap, negmu_ap, None, ALU.add))
            vstep(ve.scalar_tensor_tensor(s_c1[:, :], scr_ap, 1.0, scr_ap,
                                          ALU.mult, ALU.mult,
                                          accum_out=s_sc[:, 1:2]))
            vstep(ve.tensor_scalar(s_sc[:, 1:2], s_sc[:, 1:2], EPS * D, None,
                                   ALU.add))
            Eac.wait(sVE)
            astep(ac.activation(s_sc[:, 2:3], s_sc[:, 1:2], AF.Ln, bias=0.0,
                                scale=1.0 / D))
            astep(ac.activation(rstd_ap, s_sc[:, 2:3], AF.Exp, bias=0.0, scale=-0.5))
            Eve.wait(sAC)
            vstep(ve.tensor_scalar(out_ap, scr_ap, rstd_ap, None, ALU.mult))

        def afence():
            astep(ac.activation(s_sc[:, 6:7], s_sc[:, 6:7], AF.Exp, scale=0.0))

        def vfence():
            vstep(ve.tensor_copy(s_sc[:, 7:8], s_sc[:, 7:8]))

        vstep(ve.memset(s_ones64[:, :], 1.0))
        vstep(ve.memset(s_sc[:, :], 0.0))
        vstep(ve.memset(s_st[:, :], 0.0))
        vstep(ve.memset(s_eps[:, :], EPS))
        vstep(ve.memset(s_csh[:, :], -CSHIFT))
        for _mi in range(4):
            vstep(ve.memset(s_vaug[:, _mi * 8448:(_mi + 1) * 8448], 1.0))
        for dst, src in [(s_bq, bqt_d), (s_bk, bkt_d), (s_b1t, b1t_d), (s_br, brc_d),
                         (s_eye, eye8_d), (s_bvb, bvb_d), (s_bob, bob_d),
                         (s_esel, esel_d)]:
            sQ.tick(sy.dma_start(out=dst[:, :], in_=src), 16)
        for dst, src in [(s_gb, gb_d), (s_beb, beb_d), (s_b2b, b2b_d)]:
            sG.tick(gp.dma_start(out=dst[:, :], in_=src), 16)
        for c in range(8):
            sG.tick(gp.dma_start(out=s_wv[:, c * 1024:(c + 1) * 1024],
                                 in_=wvT[c * 128:(c + 1) * 128, :]), 16)
        for dblk in range(8):
            sG.tick(gp.dma_start(out=s_wr[:, dblk * 8:(dblk + 1) * 8],
                                 in_=wrT_d[dblk * 128:(dblk + 1) * 128, :]), 16)

        for i in range(NT_OWN):
            Esy.wait(sVE, sVE.n)
            sQ.tick(sy.dma_start(out=s_xt[:, :], in_=x_d[i * 128:(i + 1) * 128, :]), 16)
            Eve.wait(sQ)
            Eac.wait(sQ)
            ln_tile(s_xt[:, :], s_s1[:, :], s_ln1s[:, :], s_sc[:, 4:5], s_sc[:, 5:6])
            vfence()
            Esy.wait(sVE)
            sQ.tick(sy.dma_start(out=ln0_loc.ap()[i * 128:(i + 1) * 128, :],
                                 in_=s_ln1s[:, :]), 16)
        sG.tick(gp.dma_start(out=bar_in.ap(), in_=x_d[0:1, 0:64]), 16)
        Egp.wait(sQ)
        Egp.wait(sG)
        sCC.tick(gp.collective_compute(
            "AllGather", ALU.bypass, replica_groups=rg2,
            ins=[ln0_loc.ap().opt()], outs=[ln0_pr.ap().opt()]))
        Egp.wait(sCC)
        sCC.tick(gp.collective_compute(
            "AllReduce", ALU.add, replica_groups=rg8,
            ins=[bar_in.ap().opt()], outs=[bar_out.ap().opt()]))
        CC_AG0 = sCC.n

        ln0t_free = [0, 0]

        def load_ln0t(mc, buf, rows_src):
            Esy.wait(sPE, ln0t_free[buf])
            Eac.wait(sPE, ln0t_free[buf])
            base = buf * 4096
            for c in range(8):
                eng = sy
                sT[buf].tick(eng.dma_start(
                    out=s_ln0t[:, base + c * 512: base + (c + 1) * 512],
                    in_=rows_src[mc * 512:(mc + 1) * 512, c * 128:(c + 1) * 128],
                    transpose=True), 16)

        Esy.wait(sCC, CC_AG0)
        Eac.wait(sCC, CC_AG0)
        vp_evac = []
        for mc in range(8):
            buf = mc % 2
            load_ln0t(mc, buf, ln0_pr.ap())
            Epe.wait(sT[buf])
            Epe.wait(sG)
            for mt in range(4):
                m = mc * 4 + mt
                P = PA if (m % 2 == 0) else PB
                if m >= 2:
                    Epe.wait(sVE, vp_evac[m - 2])
                for c in range(8):
                    lhs = s_ln0t[:, buf * 4096 + c * 512 + mt * 128:
                                 buf * 4096 + c * 512 + (mt + 1) * 128]
                    rhs = s_wv[:, c * 1024:(c + 1) * 1024]
                    pe.matmul(P[:, 0:512], lhs, rhs[:, 0:512],
                              start=(c == 0), stop=(c == 7))
                    last = pe.matmul(P[:, 512:1024], lhs, rhs[:, 512:1024],
                                     start=(c == 0), stop=(c == 7))
                sPE.tick(last)
                Eve.wait(sPE)
                vstep(ve.tensor_add(s_s1[:, :], P[:, :], s_bvb[:, :]))
                for hh in range(16):
                    cp = ve.tensor_copy(
                        s_vaug[:, m * VSTRIDE + hh * 66: m * VSTRIDE + hh * 66 + 64],
                        s_s1[:, hh * 64:(hh + 1) * 64])
                sVE.tick(cp)
                Eve.wait(sVE)
                vfence()
                vp_evac.append(sVE.n)
            ln0t_free[buf] = sPE.n
        V_READY_VE = sVE.n

        qk_free_pe = [sPE.n]
        qk_evac = []

        def qk_proj_group(g):
            pairs = (2 * g, 2 * g + 1)
            Epe.wait(sVE, V_READY_VE)
            Egp.wait(sPE, qk_free_pe[0])
            for i, p in enumerate(pairs):
                for c in range(8):
                    sG.tick(gp.dma_start(
                        out=s_wqk[:, i * 1024 + c * 128: i * 1024 + (c + 1) * 128],
                        in_=wqT[c * 128:(c + 1) * 128, p * 128:(p + 1) * 128]), 16)
            WQ_G = sG.n
            for j in range(12):
                is_q = j < 8
                mc = j if is_q else j - 8
                buf = j % 2
                if j == 8:
                    Egp.wait(sPE)
                    for i, p in enumerate(pairs):
                        for c in range(8):
                            sG.tick(gp.dma_start(
                                out=s_wqk[:, i * 1024 + c * 128:
                                          i * 1024 + (c + 1) * 128],
                                in_=wkT[c * 128:(c + 1) * 128,
                                        p * 128:(p + 1) * 128]), 16)
                load_ln0t(mc, buf, ln0_pr.ap() if is_q else ln0_loc.ap())
                Epe.wait(sT[buf])
                Epe.wait(sG)
                for i, p in enumerate(pairs):
                    P = PA if ((j * 2 + i) % 2 == 0) else PB
                    if len(qk_evac) >= 2:
                        Epe.wait(sVE, qk_evac[-2])
                    for c in range(8):
                        last = pe.matmul(
                            P[:, 0:512],
                            s_wqk[:, i * 1024 + c * 128: i * 1024 + (c + 1) * 128],
                            s_ln0t[:, buf * 4096 + c * 512:
                                   buf * 4096 + (c + 1) * 512],
                            start=(c == 0), stop=(c == 7))
                    sPE.tick(last)
                    Eve.wait(sPE)
                    q = p % 2
                    if is_q:
                        dst = s_qt[:, q * 4096 + mc * 512: q * 4096 + (mc + 1) * 512]
                        bias = s_bq[:, p:p + 1]
                    else:
                        dst = s_kt[:, q * 2048 + mc * 512: q * 2048 + (mc + 1) * 512]
                        bias = s_bk[:, p:p + 1]
                    vstep(ve.tensor_scalar(dst, P[:, 0:512], bias, None, ALU.add))
                    qk_evac.append(sVE.n)
                ln0t_free[buf] = sPE.n
            vfence()
            qk_free_pe[0] = sPE.n

        ATTN_VE = [0]

        def attn_pair(p):
            q = p % 2
            Epe.wait(sVE)
            Epe.wait(sVE, V_READY_VE)
            for nch in range(4):
                nbase = nch * 512
                Epe.wait(sVE)
                s_done = []
                for quad in range(4):
                    for mi in range(8):
                        m = quad * 8 + mi
                        for hh in range(2):
                            k = mi * 2 + hh
                            n_op = quad * 16 + k
                            bank = SROT[n_op % 4]
                            if n_op >= 4:
                                Epe.wait(sAC, s_done[n_op - 4])
                            lhs = s_qt[hh * 64:(hh + 1) * 64,
                                       q * 4096 + m * 128: q * 4096 + (m + 1) * 128]
                            rhs = s_kt[hh * 64:(hh + 1) * 64,
                                       q * 2048 + nbase: q * 2048 + nbase + 512]
                            mm = pe.matmul(bank[:, :], lhs, rhs, start=True,
                                           stop=True, tile_position=(hh * 64, 0))
                            sPE.tick(mm)
                            Eac.wait(sPE)
                            ex = ac.activation(s_equad[:, k * 512:(k + 1) * 512],
                                               bank[:, :], AF.Exp,
                                               bias=0.0, scale=SCALE)
                            s_done.append(sAC.tick(ex))
                    afence()
                    fq = sAC.n
                    for mi in range(8):
                        m = quad * 8 + mi
                        for hh in range(2):
                            k = mi * 2 + hh
                            Epe.wait(sAC, fq)
                            lhs = s_vaug[:, m * VSTRIDE + (p * 2 + hh) * 66:
                                         m * VSTRIDE + (p * 2 + hh) * 66 + 65]
                            mm = pe.matmul(PB[0:65, hh * 512:(hh + 1) * 512], lhs,
                                           s_equad[:, k * 512:(k + 1) * 512],
                                           start=(m == 0), stop=(m == NMT - 1))
                    sPE.tick(mm)
                for hh in range(2):
                    Eve.wait(sPE)
                    vstep(ve.tensor_copy(s_s2[0:65, 0:512],
                                         PB[0:65, hh * 512:(hh + 1) * 512]))
                    with nc.allow_low_precision(
                            reason="softmax 1/sum stored bf16 for the rank-1 "
                                   "broadcast matmul; same error class as bf16 "
                                   "matmul operands"):
                        vstep(ve.reciprocal(s_r[:, :], s_s2[64:65, 0:512]))
                    vfence()
                    Epe.wait(sVE)
                    mm = pe.matmul(PC[0:64, :], s_ones64[:, :], s_r[:, :],
                                   start=True, stop=True)
                    sPE.tick(mm)
                    Eve.wait(sPE)
                    dst = s_attnT[hh * 64:(hh + 1) * 64,
                                  p * 2048 + nbase: p * 2048 + nbase + 512]
                    vstep(ve.tensor_mul(dst, s_s2[0:64, 0:512], PC[0:64, :]))
            vfence()
            ATTN_VE[0] = sVE.n

        for g in range(4):
            qk_proj_group(g)
            attn_pair(2 * g)
            attn_pair(2 * g + 1)
        QT_DEAD_PE = sPE.n

        Egp.wait(sPE, qk_free_pe[0])
        for c in range(8):
            sG.tick(gp.dma_start(out=s_wo[:, c * 1024:(c + 1) * 1024],
                                 in_=woT[c * 128:(c + 1) * 128, :]), 16)
        WO_G = sG.n
        Egp.wait(sPE, QT_DEAD_PE)
        for dblk in range(8):
            sW.tick(gp.dma_start(out=s_w1[:, dblk * 4096:(dblk + 1) * 4096],
                                 in_=w1T_d[dblk * 128:(dblk + 1) * 128, :]), 16)
        for fb in range(30):
            sW.tick(gp.dma_start(out=s_w2[:, fb * 1024:(fb + 1) * 1024],
                                 in_=w2T_d[fb * 128:(fb + 1) * 128, :]), 16)

        Epe.wait(sVE, ATTN_VE[0])
        Epe.wait(sG, WO_G)
        pa_ok = 0
        for i in range(NT_OWN):
            Epe.wait(sVE, pa_ok)
            for fb in range(8):
                lhs = s_attnT[:, fb * 2048 + i * 128: fb * 2048 + (i + 1) * 128]
                rhs = s_wo[:, fb * 1024:(fb + 1) * 1024]
                pe.matmul(PA[:, 0:512], lhs, rhs[:, 0:512],
                          start=(fb == 0), stop=(fb == 7))
                mm = pe.matmul(PA[:, 512:1024], lhs, rhs[:, 512:1024],
                               start=(fb == 0), stop=(fb == 7))
            sPE.tick(mm)
            Esy.wait(sVE)
            sQ.tick(sy.dma_start(out=s_xt[:, :], in_=x_d[i * 128:(i + 1) * 128, :]), 16)
            Eve.wait(sPE)
            Eve.wait(sQ)
            vstep(ve.tensor_add(s_s1[:, :], PA[:, :], s_bob[:, :]))
            pa_ok = sVE.n
            ln_tile(s_s1[:, :], s_s2[:, :], s_s2[:, :], s_sc[:, 4:5], s_sc[:, 5:6])
            Eve.wait(sQ)
            vstep(ve.tensor_add(s_s1[:, :], s_xt[:, :], s_s2[:, :]))
            vfence()
            Esy.wait(sVE)
            sQ.tick(sy.dma_start(out=x1sp.ap()[i * 128:(i + 1) * 128, :],
                                 in_=s_s1[:, :]), 16)
            ln_tile(s_s1[:, :], s_s2[:, :], s_ln1s[:, :],
                    s_st[:, 2 * i:2 * i + 1], s_st[:, 2 * i + 1:2 * i + 2])
            vfence()
            Esy.wait(sVE)
            sQ.tick(sy.dma_start(out=ln1_loc.ap()[i * 128:(i + 1) * 128, :],
                                 in_=s_ln1s[:, :]), 16)
        OUTPROJ_PE = sPE.n
        Egp.wait(sVE)
        for fb in (30, 31):
            sW.tick(gp.dma_start(out=s_w2[:, fb * 1024:(fb + 1) * 1024],
                                 in_=w2T_d[fb * 128:(fb + 1) * 128, :]), 16)
        W12_W = sW.n
        Egp.wait(sQ)
        sCC.tick(gp.collective_compute(
            "AllGather", ALU.bypass, replica_groups=rg8,
            ins=[ln1_loc.ap().opt()], outs=[ln1_ag.ap().opt()]))
        Egp.wait(sCC)
        sCC.tick(gp.collective_compute(
            "AllReduce", ALU.add, replica_groups=rg8,
            ins=[bar_in.ap().opt()], outs=[bar_out.ap().opt()]))
        CC_AG1 = sCC.n

        ln1t_free = [OUTPROJ_PE, OUTPROJ_PE]
        pa_free_ve = [0, 0]
        rs_done = []
        phasea_evac = []

        def load_ln1t(s, buf):
            Esy.wait(sPE, ln1t_free[buf])
            Eac.wait(sPE, ln1t_free[buf])
            for bi in range(2):
                g, r = PBLOCKS[2 * s + bi]
                rows = ln1_ag.ap()[r * TOWN + g * 256: r * TOWN + (g + 1) * 256, :]
                for c in range(8):
                    eng = sy
                    sT[buf].tick(eng.dma_start(
                        out=s_ln1t[:, buf * 4096 + c * 512 + bi * 256:
                                   buf * 4096 + c * 512 + bi * 256 + 256],
                        in_=rows[:, c * 128:(c + 1) * 128], transpose=True), 16)

        def ffn_epilogue(cidx):
            Esy.wait(sCC, rs_done[cidx])
            Eve.wait(sC[0])
            Eve.wait(sC[1])
            for k in range(2):
                i = cidx * 2 + k
                Esy.wait(sVE)
                sQ.tick(sy.dma_start(out=s_xt[:, :],
                                     in_=x1sp.ap()[i * 128:(i + 1) * 128, :]), 16)
                sQ.tick(sy.dma_start(out=s_s2[:, :],
                                     in_=rsout.ap()[i * 128:(i + 1) * 128, :]), 16)
                Eve.wait(sQ)
                Eve.wait(sST)
                vstep(ve.tensor_scalar(s_s1[:, :], s_xt[:, :],
                                       s_st[:, 2 * i:2 * i + 1],
                                       s_st[:, 2 * i + 1:2 * i + 2],
                                       ALU.add, ALU.mult))
                vstep(ve.tensor_add(s_s1[:, :], s_s1[:, :], s_s2[:, :]))
                ln_tile(s_s1[:, :], s_s2[:, :], s_s2[:, :], s_sc[:, 4:5],
                        s_sc[:, 5:6])
                vstep(ve.tensor_mul(s_s2[:, :], s_s2[:, :], s_gb[:, :]))
                vstep(ve.tensor_add(s_s2[:, :], s_s2[:, :], s_beb[:, :]))
                ln_tile(s_s2[:, :], s_s1[:, :], s_s1[:, :], s_sc[:, 4:5],
                        s_sc[:, 5:6])
                vstep(ve.tensor_add(s_s1[:, :], s_s1[:, :], s_xt[:, :]))
                vfence()
                Eac.wait(sVE)
                sST.tick(ac.dma_start(out=out_d[i * 128:(i + 1) * 128, :],
                                      in_=s_s1[:, :]), 16)

        Esy.wait(sCC, CC_AG1)
        Eac.wait(sCC, CC_AG1)
        Epe.wait(sW, W12_W)
        Epe.wait(sVE)
        Epe.wait(sAC)

        for s in range(NSUB):
            buf = s % 2
            load_ln1t(s, buf)
            Epe.wait(sT[buf])
            lnb = s_ln1t[:, buf * 4096:(buf + 1) * 4096]

            Epe.wait(sAC)
            for dblk in range(8):
                mm = pe.matmul(PE_[0:8, :], s_wr[:, dblk * 8:(dblk + 1) * 8],
                               lnb[:, dblk * 512:(dblk + 1) * 512],
                               start=(dblk == 0), stop=(dblk == 7))
            sPE.tick(mm)
            Eve.wait(sPE)
            vstep(ve.tensor_scalar(s_lgT[:, :], PE_[0:8, :], s_br[:, :], None,
                                   ALU.add))
            vfence()
            for t in range(4):
                Epe.wait(sVE)
                mm = pe.transpose(PF[:, 0:8], s_lgT[:, t * 128:(t + 1) * 128],
                                  s_eye[:, :])
                sPE.tick(mm)
                Eve.wait(sPE)
                vstep(ve.tensor_copy(s_lg[:, :], PF[:, 0:8]))
                vstep(ve.reduce_max(s_rm[:, 0:1], s_lg[:, :], axis=AX.X))
                vstep(ve.tensor_scalar(s_ra[:, :], s_lg[:, :], s_rm[:, 0:1], 0.0,
                                       ALU.subtract, ALU.is_ge))
                vstep(ve.tensor_scalar(s_rb[:, :], s_ra[:, :], -1e30, None, ALU.mult))
                vstep(ve.tensor_add(s_rb[:, :], s_lg[:, :], s_rb[:, :]))
                vstep(ve.reduce_max(s_rm[:, 1:2], s_rb[:, :], axis=AX.X))
                vstep(ve.tensor_scalar(s_rb[:, :], s_rb[:, :], s_rm[:, 1:2], 0.0,
                                       ALU.subtract, ALU.is_ge))
                vstep(ve.tensor_sub(s_rm[:, 2:3], s_rm[:, 1:2], s_rm[:, 0:1]))
                Eac.wait(sVE)
                astep(ac.activation(s_rm[:, 3:4], s_rm[:, 2:3], AF.Exp))
                Eve.wait(sAC)
                vstep(ve.tensor_scalar(s_rm[:, 4:5], s_rm[:, 3:4], 1.0, None,
                                       ALU.add))
                vstep(ve.reciprocal(s_rm[:, 5:6], s_rm[:, 4:5]))
                vstep(ve.tensor_mul(s_rm[:, 6:7], s_rm[:, 3:4], s_rm[:, 5:6]))
                vstep(ve.tensor_scalar(s_ra[:, :], s_ra[:, :], s_rm[:, 5:6], None,
                                       ALU.mult))
                vstep(ve.tensor_scalar(s_rb[:, :], s_rb[:, :], s_rm[:, 6:7], None,
                                       ALU.mult))
                vstep(ve.tensor_add(s_ra[:, :], s_ra[:, :], s_rb[:, :]))
                vstep(ve.tensor_mul(s_ra[:, :], s_ra[:, :], s_esel[:, :]))
                vstep(ve.reduce_sum(s_wcol[:, t:t + 1], s_ra[:, :], axis=AX.X))

            for fb in range(32):
                bank = PC if (fb % 2 == 0) else PD
                if len(phasea_evac) >= 2:
                    Epe.wait(sVE, phasea_evac[-2])
                for dblk in range(8):
                    mm = pe.matmul(bank[:, :],
                                   s_w1[:, dblk * 4096 + fb * 128:
                                        dblk * 4096 + (fb + 1) * 128],
                                   lnb[:, dblk * 512:(dblk + 1) * 512],
                                   start=(dblk == 0), stop=(dblk == 7))
                sPE.tick(mm)
                Eve.wait(sPE)
                vstep(ve.tensor_scalar(s_ht[:, fb * 512:(fb + 1) * 512], bank[:, :],
                                       s_b1t[:, fb:fb + 1], 0.0,
                                       ALU.add, ALU.max))
                phasea_evac.append(sVE.n)
            vfence()

            for t in range(4):
                P = PA if (t % 2 == 0) else PB
                Epe.wait(sVE, pa_free_ve[t % 2])
                Epe.wait(sVE)
                for fb in range(32):
                    lhs = s_ht[:, fb * 512 + t * 128: fb * 512 + (t + 1) * 128]
                    rhs = s_w2[:, fb * 1024:(fb + 1) * 1024]
                    pe.matmul(P[:, 0:512], lhs, rhs[:, 0:512],
                              start=(fb == 0), stop=(fb == 31))
                    mm = pe.matmul(P[:, 512:1024], lhs, rhs[:, 512:1024],
                                   start=(fb == 0), stop=(fb == 31))
                sPE.tick(mm)
                Eve.wait(sPE)
                Eve.wait(sC[(s // 4) % 2])
                Eve.wait(sC[(s // 4 + 1) % 2])
                vstep(ve.tensor_add(s_c1[:, :], P[:, :], s_b2b[:, :]))
                vstep(ve.tensor_scalar(s_c1[:, :], s_c1[:, :], s_wcol[:, t:t + 1],
                                       None, ALU.mult))
                vfence()
                pa_free_ve[t % 2] = sVE.n
                rowbase = (2 * s + t // 2) * 256 + (t % 2) * 128
                Esy.wait(sVE)
                sC[(s // 4) % 2].tick(
                    sy.dma_start(out=contrib.ap()[rowbase:rowbase + 128, :],
                                 in_=s_c1[:, :]), 16)
            ln1t_free[buf] = sPE.n

            if s % 4 == 3:
                cidx = s // 4
                Egp.wait(sC[cidx % 2])
                sCC.tick(gp.collective_compute(
                    "ReduceScatter", ALU.add, replica_groups=rg8,
                    ins=[contrib.ap()[cidx * 2048:(cidx + 1) * 2048, :].opt()],
                    outs=[rsout.ap()[cidx * 256:(cidx + 1) * 256, :].opt()]))
                rs_done.append(sCC.n)
                if cidx >= 1:
                    ffn_epilogue(cidx - 1)
        ffn_epilogue(7)
        Egp.wait(sST, 16 * 16)
        for eng_px in (sy, gp, pe, ac, ve):
            eng_px.flush()

    return nc


_NC_CACHE = []


def kernel(**inputs):
    def f(k):
        return np.ascontiguousarray(np.asarray(inputs[k], dtype=np.float32))

    x = f("L0_attn_x").reshape(NCORE * TOWN, D)
    wq, wk, wv, wo = f("L0_attn_Wq"), f("L0_attn_Wk"), f("L0_attn_Wv"), f("L0_attn_Wo")
    bq, bk, bv, bo = f("L0_attn_bq"), f("L0_attn_bk"), f("L0_attn_bv"), f("L0_attn_bo")
    wr, br = f("L0_ffn_Wr"), f("L0_ffn_br")
    w1, b1 = f("L0_ffn_W1"), f("L0_ffn_b1")
    w2, b2 = f("L0_ffn_W2"), f("L0_ffn_b2")
    gam, bet = f("L0_ffn_gamma"), f("L0_ffn_beta")

    wqT = np.ascontiguousarray(wq.T)
    wkT = np.ascontiguousarray(wk.T)
    wvT = np.ascontiguousarray(wv.T)
    woT = np.ascontiguousarray(wo.T)
    wrT = np.ascontiguousarray(wr.T)
    bqt = np.ascontiguousarray(bq.reshape(8, 128).T)
    bkt = np.ascontiguousarray(bk.reshape(8, 128).T)
    bvb = np.ascontiguousarray(np.broadcast_to(bv, (128, D)))
    bob = np.ascontiguousarray(np.broadcast_to(bo, (128, D)))
    gb = np.ascontiguousarray(np.broadcast_to(gam, (128, D)))
    beb = np.ascontiguousarray(np.broadcast_to(bet, (128, D)))
    brc = np.ascontiguousarray(br.reshape(8, 1))
    eye8 = np.eye(8, dtype=np.float32)

    if not _NC_CACHE:
        _NC_CACHE.append(build_nc())
    nc = _NC_CACHE[0]

    in_maps = []
    for c in range(NCORE):
        esel = np.zeros((128, 8), np.float32)
        esel[:, c] = 1.0
        in_maps.append(dict(
            x=np.ascontiguousarray(x[c * TOWN:(c + 1) * TOWN]),
            wqT=wqT, wkT=wkT, wvT=wvT, woT=woT,
            w1T=np.ascontiguousarray(w1[c].T),
            w2T=np.ascontiguousarray(w2[c].T),
            wrT=wrT, bqt=bqt, bkt=bkt,
            b1t=np.ascontiguousarray(b1[c].reshape(32, 128).T),
            brc=brc, bvb=bvb, bob=bob, gb=gb, beb=beb,
            b2b=np.ascontiguousarray(np.broadcast_to(b2[c], (128, D))),
            eye8=eye8, esel=esel,
        ))
    res = run_bass_kernel_spmd(nc, in_maps, core_ids=list(range(NCORE)))
    out = np.concatenate([np.asarray(res.results[c]["out"]) for c in range(NCORE)],
                         axis=0)
    return out.reshape(4, SEQ, D).astype(np.float32)
